# version 18
# baseline (speedup 1.0000x reference)
import numpy as np
import ml_dtypes
import concourse.bass as bass
import concourse.mybir as mybir
from concourse.bass_utils import run_bass_kernel_spmd

F32 = mybir.dt.float32
BF16 = mybir.dt.bfloat16
I32 = mybir.dt.int32
Alu = mybir.AluOpType
Act = mybir.ActivationFunctionType
Ax = mybir.AxisListType
BF = ml_dtypes.bfloat16

D = 1024
T = 2048
NCH = 8
NTS = 16
NTT = 4
NH = 16
HD = 64
NP = 8
SCALE = HD ** -0.5
NEX = 18
NFC = 24
MASKV = -1e30

ENGS = ('t', 'v', 's', 'g', 'q')


class PB:
    def __init__(self, nc):
        self.nc = nc
        self.fns = {e: [] for e in ENGS}
        self.cnt = {k: 0 for k in ENGS}
        self.chcnt = {}
        self.waited = {e: {} for e in ENGS}
        self.pos = {e: 0 for e in ENGS}
        self.dr = {e: 0 for e in ENGS}
        self.sems = None

    def _flat(self, tok):
        if tok is None:
            return
        if isinstance(tok, tuple):
            yield tok
            return
        for t in tok:
            yield from self._flat(t)

    def wait(self, e, tok):
        need = {}
        for sname, c, src, spos in self._flat(tok):
            if src == e:
                if e == 't':
                    continue
                if self.dr[e] < spos:
                    self.dr[e] = self.pos[e]
                    self.pos[e] += 1
                    self.fns[e].append(lambda eng: eng.drain())
            else:
                if self.waited[e].get(sname, 0) >= c:
                    continue
                if need.get(sname, 0) < c:
                    need[sname] = c
        for sname, c in need.items():
            self.waited[e][sname] = c
            self.fns[e].append(
                lambda eng, s=sname, cc=c: eng.wait_ge(self.sems[s], cc))

    def emit(self, e, fn, wait=None, inc=False, dma=False, ch=None):
        if wait is not None:
            self.wait(e, wait)
        self.pos[e] += 1
        mypos = self.pos[e]
        if inc:
            if dma:
                sname = 'D_' + ch
                step = 16
                self.chcnt[sname] = self.chcnt.get(sname, 0) + step
                c = self.chcnt[sname]
            else:
                sname = e
                step = 1
                self.cnt[sname] += step
                c = self.cnt[sname]
            self.fns[e].append(
                lambda eng, f=fn, s=sname, st=step: f(eng).then_inc(self.sems[s], st))
            return (sname, c, sname if dma else e, mypos)
        self.fns[e].append(fn)
        return None

    def barrier(self, skip_dma=()):
        toks = []
        for e in ENGS:
            toks.append(self.emit(e, lambda eng: eng.drain(), inc=True))
        for s, c in self.chcnt.items():
            if c and s not in skip_dma:
                toks.append((s, c, s, 0))
        for e in ENGS:
            self.wait(e, toks)


class B:
    __slots__ = ('wr', 'rd')

    def __init__(self):
        self.wr = None
        self.rd = []

    def w(self):
        out = [t for t in self.rd if t is not None]
        if self.wr is not None:
            out.append(self.wr)
        return out

    def setw(self, tok):
        self.wr = tok
        self.rd = []

    def addr(self, tok):
        if tok is not None:
            self.rd.append(tok)


def _pairs_layout(w):
    return np.ascontiguousarray(
        w.T.reshape(8, 128, 8, 128).transpose(2, 1, 0, 3).reshape(8, 128, 1024))


def prep_host(ins):
    d = {}
    for L, pre in ((0, 'L0_attn_'), (1, 'L1_attn_')):
        d[f'wq{L}'] = _pairs_layout(np.asarray(ins[pre + 'Wq']))
        d[f'wk{L}'] = _pairs_layout(np.asarray(ins[pre + 'Wk']))
        d[f'wv{L}'] = _pairs_layout(np.asarray(ins[pre + 'Wv']))
        d[f'wo{L}'] = np.ascontiguousarray(
            np.asarray(ins[pre + 'Wo']).T.reshape(8, 128, 1024)
            .transpose(1, 0, 2).reshape(128, 8192))
    bias_pp = np.zeros((128, 32), np.float32)
    bias_row = np.zeros((1, 4096), np.float32)
    for L, pre in ((0, 'L0_attn_'), (1, 'L1_attn_')):
        bias_pp[:, L * 16:L * 16 + 8] = np.asarray(ins[pre + 'bq']).reshape(8, 128).T
        bias_pp[:, L * 16 + 8:L * 16 + 16] = np.asarray(ins[pre + 'bk']).reshape(8, 128).T
        bias_row[0, L * 2048:L * 2048 + 1024] = np.asarray(ins[pre + 'bv'])
        bias_row[0, L * 2048 + 1024:L * 2048 + 2048] = np.asarray(ins[pre + 'bo'])
    d['bias_pp'] = bias_pp
    d['bias_row'] = bias_row

    wc = np.asarray(ins['L0_attn_W_cross'])
    sw = np.asarray(ins['L0_attn_shift_weights'])
    tau = np.arange(HD)[None, :] - np.arange(HD)[:, None]
    valid = (tau >= 0) & (tau < 32)
    m_all = np.where(valid[None], sw[:, np.clip(tau, 0, 31)], np.float32(0.0)).astype(np.float32)
    wcb = np.zeros((128, 1024), np.float32)
    mbk = np.zeros((128, 1024), np.float32)
    for h in range(16):
        pr, half = h // 2, h % 2
        r0, c0 = half * 64, pr * 128 + half * 64
        wcb[r0:r0 + 64, c0:c0 + 64] = wc[h]
        mbk[r0:r0 + 64, c0:c0 + 64] = m_all[h]
    d['wcb'] = wcb
    d['mblk'] = mbk
    u16 = np.zeros((16, 128), np.float32)
    e5 = np.zeros((16, 1024), np.float32)
    for h in range(16):
        u16[h, :] = ((np.arange(128) >= 64) == (h % 2)).astype(np.float32)
        e5[h, (h // 2) * 128:(h // 2) * 128 + 128] = 1.0
    d['u16'] = u16
    d['e5'] = e5
    d['gate'] = np.asarray(ins['L0_attn_gate']).reshape(16, 1).astype(np.float32)

    d['ident'] = np.eye(128, dtype=np.float32)
    d['atri'] = np.asarray(np.triu(np.ones((128, 128), np.float32)), dtype=BF)
    ii = np.arange(128)[:, None]
    cc = np.arange(896)[None, :]
    m = np.where(ii > cc - 384, MASKV, 0.0)
    bm0 = np.zeros((128, 896), np.float32)
    bm0[0] = m[0]
    bm0[1:] = m[1:] - m[:-1]
    d['bm0'] = np.asarray(bm0, dtype=BF)
    cc = np.arange(1152)[None, :]
    m = np.where(np.abs(ii - cc + 512) > 3, MASKV, 0.0)
    bm1 = np.zeros((128, 1152), np.float32)
    bm1[0] = m[0]
    bm1[1:] = m[1:] - m[:-1]
    d['bm1'] = np.asarray(bm1, dtype=BF)
    d['ones_row'] = np.ones((1, 128), np.float32)
    d['ones2d'] = np.ones((128, 64), np.float32)
    eidx = np.zeros((128, 256), np.float32)
    for e in range(16):
        eidx[:, e::16] = e
    d['eidx'] = eidx
    d['tokids'] = (np.arange(16)[None, :] * 128 +
                   np.arange(128)[:, None]).astype(np.uint32)
    d['padfill'] = (2048 + np.arange(12288, dtype=np.uint32) // 16).reshape(16, 768)

    w1 = np.asarray(ins['L0_ffn_w1_e'])
    w3 = np.asarray(ins['L0_ffn_w3_e'])
    w2 = np.asarray(ins['L0_ffn_w2_e'])
    w1s = np.asarray(ins['L0_ffn_w1_s'])
    w3s = np.asarray(ins['L0_ffn_w3_s'])
    w2s = np.asarray(ins['L0_ffn_w2_s'])

    def exp_in(w):
        return np.ascontiguousarray(
            w.T.reshape(8, 128, 512).transpose(1, 0, 2).reshape(128, 4096))

    def exp_out(w):
        return np.ascontiguousarray(
            w.T.reshape(4, 128, 1024).transpose(1, 0, 2).reshape(128, 4096))

    wa = np.zeros((NEX, 128, 4096), np.float32)
    wb = np.zeros((NEX, 128, 4096), np.float32)
    wc2 = np.zeros((NEX, 128, 4096), np.float32)
    for e in range(16):
        wa[e] = exp_in(w1[e])
        wb[e] = exp_in(w3[e])
        wc2[e] = exp_out(w2[e])
    for hf in range(2):
        wa[16 + hf] = exp_in(w1s[hf * 512:(hf + 1) * 512, :])
        wb[16 + hf] = exp_in(w3s[hf * 512:(hf + 1) * 512, :])
        wc2[16 + hf] = exp_out(w2s[:, hf * 512:(hf + 1) * 512])
    d['moe_w1'] = wa
    d['moe_w3'] = wb
    d['moe_w2'] = wc2
    d['wg'] = np.ascontiguousarray(
        np.asarray(ins['L0_ffn_w_gate']).T.reshape(8, 128, 16)
        .transpose(1, 0, 2).reshape(128, 128))

    d['wg1'] = np.ascontiguousarray(np.asarray(ins['L1_ffn_w_gate'])
                                    .T.reshape(8, 128, 24, 128).transpose(2, 1, 0, 3)
                                    .reshape(24, 128, 1024))
    d['wu1'] = np.ascontiguousarray(np.asarray(ins['L1_ffn_w_up'])
                                    .T.reshape(8, 128, 24, 128).transpose(2, 1, 0, 3)
                                    .reshape(24, 128, 1024))
    d['wd1'] = np.ascontiguousarray(np.asarray(ins['L1_ffn_w_down'])
                                    .T.reshape(24, 128, 1024).transpose(1, 0, 2)
                                    .reshape(128, 24576))
    return d


def build_program(dbg=(), phases=('ln0', 'at0', 'moe', 'ep0', 'ln1', 'at1', 'f1')):
    nc = bass.Bass("TRN2", target_bir_lowering=False)
    pb = PB(nc)
    H = {}

    def din(name, shape, dt):
        H[name] = nc.dram_tensor(name, shape, dt, kind="ExternalInput").ap()

    din('x_in', [T, D], F32)
    for L in (0, 1):
        for w in ('wq', 'wk', 'wv'):
            din(f'{w}{L}', [8, 128, 1024], F32)
        din(f'wo{L}', [128, 8192], F32)
    din('bias_pp', [128, 32], F32)
    din('bias_row', [1, 4096], F32)
    din('wcb', [128, 1024], F32)
    din('mblk', [128, 1024], F32)
    din('u16', [16, 128], F32)
    din('e5', [16, 1024], F32)
    din('gate', [16, 1], F32)
    din('ident', [128, 128], F32)
    din('atri', [128, 128], BF16)
    din('bm0', [128, 896], BF16)
    din('bm1', [128, 1152], BF16)
    din('ones_row', [1, 128], F32)
    din('ones2d', [128, 64], F32)
    din('moe_w1', [NEX, 128, 4096], F32)
    din('moe_w3', [NEX, 128, 4096], F32)
    din('moe_w2', [NEX, 128, 4096], F32)
    din('wg', [128, 128], F32)
    din('eidx', [128, 256], F32)
    din('tokids', [128, 16], mybir.dt.uint32)
    din('padfill', [16, 768], mybir.dt.uint32)
    din('wg1', [24, 128, 1024], I32)
    din('wu1', [24, 128, 1024], I32)
    din('wd1', [128, 24576], I32)

    out_h = nc.dram_tensor('out', [T, D], F32, kind="ExternalOutput").ap()
    xn_tm = nc.dram_tensor('xn_tm', [T + 768, D], F32, kind="Internal").ap()
    comb_h = nc.dram_tensor('comb_h', [T + 768, 16], F32, kind="Internal").ap()
    routed_h = nc.dram_tensor('routed_h', [T + 768, D], F32, kind="Internal").ap()
    idxl_h = nc.dram_tensor('idxl_h', [12288, 1], mybir.dt.uint32, kind="Internal").ap()
    x1_sp = nc.dram_tensor('x1_spill', [T, D], F32, kind="Internal").ap()
    s_sp = nc.dram_tensor('s_spill', [T, D], F32, kind="Internal").ap()
    wsp_hi = nc.dram_tensor('wsp_hi', [54, 128, 4096], BF16, kind="Internal").ap()
    wsp_lo = nc.dram_tensor('wsp_lo', [54, 128, 4096], BF16, kind="Internal").ap()
    dbg_h = {}
    for name, shape in dbg:
        dbg_h[name] = nc.dram_tensor('dbg_' + name, list(shape), F32,
                                     kind="ExternalOutput").ap()

    from contextlib import ExitStack
    with ExitStack() as es:
        def sbt(name, shape, dt):
            return es.enter_context(nc.sbuf_tensor(name, shape, dt))

        xn_hi = sbt('xn_hi', [128, NCH * T], BF16)
        xn_lo = sbt('xn_lo', [128, NCH * T], BF16)
        OR = sbt('OR', [128, 16384], F32)
        R2 = sbt('R2', [128, 12832], F32)
        R3 = sbt('R3', [128, 3584], F32)
        R4 = sbt('R4', [128, 672], F32)
        ident = sbt('c_ident', [128, 128], F32)
        atri = sbt('c_atri', [128, 128], BF16)
        bm0 = sbt('c_bm0', [128, 896], BF16)
        bm1 = sbt('c_bm1', [128, 1152], BF16)
        ones2d = sbt('c_ones2d', [128, 64], F32)
        bipp = sbt('c_bipp', [128, 32], F32)
        ahi = sbt('a_hi', [128, 1024], BF16)
        alo = sbt('a_lo', [128, 1024], BF16)
        u16 = R4[0:16, 288:416]
        gt16 = R4[0:16, 416:417]
        e5g = None
        adt = None
        wg_hi = sbt('wg_hi', [128, 128], BF16)
        wg_lo = sbt('wg_lo', [128, 128], BF16)
        comb = sbt('comb', [128, 256], F32)
        stats = sbt('stats', [128, 32], F32)

        psum = [es.enter_context(nc.psum_tensor(f'ps{i}', [128, 512], F32))
                for i in range(8)]
        pbank = [B() for _ in range(8)]

        sems = {}
        for sname in ENGS:
            sems[sname] = es.enter_context(nc.semaphore('sem_' + sname))
        pb.sems = sems

        bufs = {}

        def buf(name):
            if name not in bufs:
                bufs[name] = B()
            return bufs[name]

        def dma(qe, out_ap, in_ap, ch='misc', wait=None):
            return pb.emit(qe, lambda e, o=out_ap, i=in_ap: e.dma_start(out=o, in_=i),
                           wait=wait, inc=True, dma=True, ch=ch)

        def vop(fn, wait=None):
            return pb.emit('v', fn, wait=wait, inc=True)

        def sop(fn, wait=None):
            return pb.emit('s', fn, wait=wait, inc=True)

        def split(src_ap, hi_ap, lo_ap, wait=None):
            t1 = sop(lambda s, s_=src_ap, h=hi_ap: s.copy(h, s_), wait=wait)
            return vop(lambda v, s_=src_ap, h=hi_ap, l=lo_ap:
                       v.tensor_tensor(l, s_, h, Alu.subtract), wait=t1)

        def mm3(out_ap, lh, ll, rh, rl, start, stop, wait=None, tile=None):
            kw = {} if tile is None else {'tile_position': tile}
            pb.emit('t', lambda t, o=out_ap, a=lh, b=rh, k=kw:
                    t.matmul(o, a, b, start=start, stop=False, **k), wait=wait)
            pb.emit('t', lambda t, o=out_ap, a=lh, b=rl, k=kw:
                    t.matmul(o, a, b, start=False, stop=False, **k))
            return pb.emit('t', lambda t, o=out_ap, a=ll, b=rh, k=kw:
                           t.matmul(o, a, b, start=False, stop=stop, **k), inc=True)

        def mm2(out_ap, lhs, rh, rl, start, stop, wait=None):
            pb.emit('t', lambda t, o=out_ap, a=lhs, b=rh:
                    t.matmul(o, a, b, start=start, stop=False), wait=wait)
            return pb.emit('t', lambda t, o=out_ap, a=lhs, b=rl:
                           t.matmul(o, a, b, start=False, stop=stop), inc=True)

        def mm2L(out_ap, lh, ll, rhs, start, stop, wait=None):
            pb.emit('t', lambda t, o=out_ap, a=lh, b=rhs:
                    t.matmul(o, a, b, start=start, stop=False), wait=wait)
            return pb.emit('t', lambda t, o=out_ap, a=ll, b=rhs:
                           t.matmul(o, a, b, start=False, stop=stop), inc=True)

        def mm1(out_ap, lh, rh, start, stop, wait=None, tile=None):
            kw = {} if tile is None else {'tile_position': tile}
            return pb.emit('t', lambda t, o=out_ap, a=lh, b=rh, k=kw:
                           t.matmul(o, a, b, start=start, stop=stop, **k),
                           wait=wait, inc=True)

        def tap(name, src_ap):
            if name in dbg_h:
                return dma('g', dbg_h[name][:, :], src_ap, ch='tap')
            return None

        ep_st = {}
        prep_tok = {}
        bg_q = []
        sstg = [R4[:, 288:416], R4[:, 416:544]]
        sqh = R4[:, 544:608].bitcast(BF16)
        sql = R4[:, 608:672].bitcast(BF16)

        def _sprep_step(e, mi, nm, hsrc, ck):
            sl = ck % 2
            sb = buf(f'sp_stg{sl}')
            t_l = dma('q', sstg[sl], hsrc[e][:, ck * 128:(ck + 1) * 128],
                      ch=f'wspl{sl}', wait=sb.w())
            sb.setw(t_l)
            hb = buf('sp_qh')
            t_h = sop(lambda s, sl=sl: s.copy(sqh, sstg[sl]), wait=[t_l] + hb.w())
            hb.setw(t_h)
            sb.addr(t_h)
            lb = buf('sp_ql')
            t_s = vop(lambda v, sl=sl: v.tensor_tensor(sql, sstg[sl], sqh,
                                                       Alu.subtract),
                      wait=[t_h] + lb.w())
            lb.setw(t_s)
            sb.addr(t_s)
            hb.addr(t_s)
            t_sh = dma('s', wsp_hi[e * 3 + mi][:, ck * 128:(ck + 1) * 128], sqh,
                       ch='wsps0', wait=[t_h])
            hb.addr(t_sh)
            t_sl2 = dma('g', wsp_lo[e * 3 + mi][:, ck * 128:(ck + 1) * 128], sql,
                        ch='wsps1', wait=[t_s])
            lb.addr(t_sl2)
            prep_tok[(e, nm)][0].append(t_sh)
            prep_tok[(e, nm)][1].append(t_sl2)

        def make_sprep(e):
            for mi, nm in enumerate(('w1', 'w3', 'w2')):
                prep_tok[(e, nm)] = ([], [])
                for ck in range(32):
                    bg_q.append(lambda e=e, mi=mi, nm=nm, ck=ck:
                                _sprep_step(e, mi, nm, H['moe_' + nm], ck))

        def pop_bg(n):
            for _ in range(n):
                if bg_q:
                    bg_q.pop(0)()

        e5g = R3[0:16, 2048:3072]
        adt = R2[:, 128:1152]
        ct = []
        for n, dst in (('ident', ident[:, :]), ('atri', atri[:, :]), ('bm0', bm0[:, :]),
                       ('bm1', bm1[:, :]), ('bias_pp', bipp[:, :]),
                       ('u16', u16[:, :]), ('gate', gt16[:, :])):
            ct.append(dma('q', dst, H[n][:, :], ch='const'))
        ct.append(dma('q', ones2d[:, :], H['ones2d'][:, :], ch='const'))
        wcb_s = R3[:, 0:1024]
        mblk_s = R3[:, 1024:2048]
        ct.append(dma('q', wcb_s, H['wcb'][:, :], ch='const'))
        ct.append(dma('q', mblk_s, H['mblk'][:, :], ch='const'))
        wgs = R2[:, 0:128]
        ct.append(dma('q', wgs, H['wg'][:, :], ch='const'))
        t_eps = [vop(lambda v, k=k: v.memset(stats[:, k * 8 + 7:k * 8 + 8], 1e-5))
                 for k in range(4)]
        ct.append(dma('q', e5g, H['e5'][:, :], ch='const'))
        t_sig = sop(lambda s: s.activation(gt16[:, 0:1], gt16[:, 0:1], Act.Sigmoid), wait=ct)
        t_e5g = vop(lambda v: v.tensor_scalar_mul(e5g, e5g, gt16[:, 0:1]),
                    wait=[t_sig] + ct)
        tg1 = mm1(psum[0][:, :], u16[:, :], e5g[:, 0:512], True, True, wait=[t_e5g] + ct)
        tg2 = mm1(psum[1][:, :], u16[:, :], e5g[:, 512:1024], True, True)
        t_d = vop(lambda v: v.tensor_tensor(adt, wcb_s, mblk_s, Alu.subtract), wait=ct)
        t_a1 = vop(lambda v: v.tensor_tensor(R3[:, 2048:2560], psum[0][:, :],
                                             adt[:, 0:512], Alu.mult), wait=[t_d, tg1, tg2])
        t_a2 = vop(lambda v: v.tensor_tensor(R3[:, 2560:3072], psum[1][:, :],
                                             adt[:, 512:1024], Alu.mult), wait=tg2)
        t_a3 = vop(lambda v: v.tensor_tensor(adt, R3[:, 2048:3072], mblk_s, Alu.add),
                   wait=[t_a1, t_a2])
        t_asp = split(adt, ahi[:, :], alo[:, :], wait=t_a3)
        t_wg = split(wgs, wg_hi[:, :], wg_lo[:, :], wait=ct)
        t_zp = vop(lambda v: v.memset(OR[:, 0:1024], 0.0))
        zp_toks = []
        for i in range(6):
            zp_toks.append(dma('s', xn_tm[T + i * 128:T + (i + 1) * 128, :],
                               OR[:, 0:1024], ch='zpad', wait=[t_zp] + zp_toks))
            zp_toks.append(dma('s', comb_h[T + i * 128:T + (i + 1) * 128, :],
                               OR[:, 0:16], ch='zpad', wait=[t_zp] + zp_toks))
        buf('or_routed').setw(zp_toks)
        pbank[0].setw(tg1); pbank[0].addr(t_a1)
        pbank[1].setw(tg2); pbank[1].addr(t_a2)
        pb.barrier()

        def phase_ln(L):
            src = H['x_in'] if L == 0 else x1_sp
            xt = [R3[:, 0:1024], R3[:, 1024:2048]]
            xnf_ = [R3[:, 2048:3072], OR[:, 0:1024]]
            ln_done = []
            for ts in range(NTS):
                ib = ts % 2
                k8 = (ts % 4) * 8
                st = stats[:, k8:k8 + 8]
                xnf = xnf_[ib]
                xb = buf(f'ln_x{ib}')
                tok_l = dma('s', xt[ib], src[ts * 128:(ts + 1) * 128, :], ch=f'lnx{ib}',
                            wait=xb.w())
                xb.setw(tok_l)
                stb = buf(f'ln_stats{ts % 4}')
                xnb = buf(f'ln_xnf{ib}')
                t_sum = vop(lambda v, x=xt[ib], st=st: v.reduce_sum(st[:, 0:1], x,
                                                                   axis=Ax.X),
                            wait=[tok_l] + stb.w())
                t_sq = sop(lambda s, x=xt[ib], xnf=xnf: s.activation(xnf, x, Act.Square),
                           wait=[tok_l] + xnb.w())
                t_ssq = vop(lambda v, st=st, xnf=xnf: v.reduce_sum(st[:, 1:2], xnf,
                                                                  axis=Ax.X), wait=t_sq)
                t_m = vop(lambda v, st=st: v.tensor_scalar_mul(st[:, 2:3], st[:, 0:1],
                                                               1.0 / D),
                          wait=t_sum)
                t_m2 = vop(lambda v, st=st: v.tensor_tensor(st[:, 3:4], st[:, 2:3],
                                                            st[:, 2:3], Alu.mult),
                           wait=t_m)
                t_v = vop(lambda v, st=st: v.scalar_tensor_tensor(st[:, 4:5], st[:, 1:2],
                                                                  1.0 / D, st[:, 3:4],
                                                                  Alu.mult, Alu.subtract),
                          wait=[t_ssq, t_m2])
                t_sd = sop(lambda s, st=st: s.activation(st[:, 5:6], st[:, 4:5], Act.Sqrt,
                                                         bias=st[:, 7:8]), wait=t_v)
                t_r = vop(lambda v, st=st: v.reciprocal(st[:, 6:7], st[:, 5:6]), wait=t_sd)
                stb.setw(t_r)
                t_xn = vop(lambda v, x=xt[ib], st=st, xnf=xnf:
                           v.tensor_scalar(xnf, x, st[:, 2:3], st[:, 6:7],
                                           Alu.subtract, Alu.mult),
                           wait=[t_r, tok_l] + xnb.w())
                xnb.setw(t_xn)
                xb.addr(t_xn)
                stb.addr(t_xn)
                if L == 0:
                    t_sp0 = dma('s', xn_tm[ts * 128:(ts + 1) * 128, :], xnf,
                                ch=f'xnsp{ib}', wait=t_xn)
                    xnb.addr(t_sp0)
                    ln_done.append(t_sp0)
                sp_list = []
                for c in range(NCH):
                    bk = c % 2
                    t_tp = pb.emit('t', lambda t, c=c, b=bk, xnf=xnf:
                                   t.transpose(psum[b][:, 0:128],
                                               xnf[:, c * 128:(c + 1) * 128], ident[:, :]),
                                   wait=[t_xn] + pbank[bk].w(), inc=True)
                    pbank[bk].setw(t_tp)
                    xnb.addr(t_tp)
                    col = c * T + ts * 128
                    t_sp = split(psum[bk][:, 0:128],
                                 xn_hi[:, col:col + 128], xn_lo[:, col:col + 128],
                                 wait=t_tp)
                    pbank[bk].addr(t_sp)
                    sp_list.append(t_sp)
                buf(f'xn_ts{ts}').setw(sp_list)
            buf('xn_ready').setw(t_sp)
            ln_done.append(t_sp)
            return ln_done

        at_stage_cache = {}

        def stage_attn_w(L, pair):
            wstage_ = R2[:, 9248:9760]
            wtok = {}
            for i, (m, hk) in enumerate((('q', 'wq'), ('k', 'wk'), ('v', 'wv'))):
                whi_ = R2[:, 9760 + i * 512:10272 + i * 512].bitcast(BF16)
                wlo_ = R2[:, 11296 + i * 512:11808 + i * 512].bitcast(BF16)
                hsrc = H[f'{hk}{L}']
                tsl = []
                for hf in range(2):
                    wsb = buf('at_wstage')
                    tl = dma('q', wstage_, hsrc[pair][:, hf * 512:(hf + 1) * 512],
                             ch='atw',
                             wait=wsb.w() + (buf(f'at_w{m}').w() if hf == 0 else []))
                    wsb.setw(tl)
                    tsp = split(wstage_, whi_[:, hf * 512:(hf + 1) * 512],
                                wlo_[:, hf * 512:(hf + 1) * 512], wait=tl)
                    wsb.addr(tsp)
                    tsl.append(tsp)
                buf(f'at_w{m}').setw(tsl)
                wtok[m] = tsl
            return wtok

        def phase_attn(L, o_guard=(), p_guard=()):
            stag = [R2[:, 0:512], R2[:, 512:1024]]
            qt_hi = R2[:, 1024:2048].bitcast(BF16)
            qt_lo = R2[:, 2048:3072].bitcast(BF16)
            k_hi = R2[:, 3072:4096].bitcast(BF16)
            k_lo = R2[:, 4096:5120].bitcast(BF16)
            qp_hi = R2[:, 5120:6144].bitcast(BF16)
            qp_lo = R2[:, 6144:7168].bitcast(BF16)
            v_hi = R2[:, 7168:8208].bitcast(BF16)
            v_lo = R2[:, 8208:9248].bitcast(BF16)
            wstage = R2[:, 9248:9760]
            w_hi = {}
            w_lo = {}
            for i, m in enumerate(('q', 'k', 'v')):
                w_hi[m] = R2[:, 9760 + i * 512:10272 + i * 512].bitcast(BF16)
                w_lo[m] = R2[:, 11296 + i * 512:11808 + i * 512].bitcast(BF16)
            vf32 = R2[:, 0:128]
            bvb = R2[:, 896:1024]
            O_hi = OR[:, 0:8192].bitcast(BF16)
            O_lo = OR[:, 8192:16384].bitcast(BF16)
            pf = [R3[:, 0:512], R3[:, 512:1024]]
            p_hi = [[R3[:, 1024 + h * 1024 + j * 512:1024 + h * 1024 + j * 512 + 256]
                     .bitcast(BF16) for j in range(2)] for h in range(2)]
            p_lo = [[R3[:, 1280 + h * 1024 + j * 512:1280 + h * 1024 + j * 512 + 256]
                     .bitcast(BF16) for j in range(2)] for h in range(2)]
            recs_t = R3[:, 3072:3584]
            o_un = R3[:, 0:512]
            o_f32 = R3[:, 512:1024]
            bmask = bm0 if L == 0 else bm1
            wqh, wkh, wvh = H[f'wq{L}'], H[f'wk{L}'], H[f'wv{L}']

            def kts_of(qt):
                if L == 0:
                    return list(range(0, 4 * qt + 4))
                return list(range(max(0, 4 * qt - 1), min(16, 4 * qt + 5)))

            def mclass(qt, kt):
                if L == 0:
                    dd = 4 * qt - kt
                    return None if dd >= 1 else (3 + dd) * 128
                return (5 - (kt - 4 * qt + 1)) * 128

            for pair in range(NP):
                pd_prev = buf('at_pairdone').wr
                wtok = at_stage_cache.pop((L, pair), None)
                if wtok is None:
                    wtok = stage_attn_w(L, pair)
                pop_bg(6)
                for m, dhi, dlo, bcol, dname in (('k', k_hi, k_lo, 8, 'at_kd'),
                                                 ('q', qt_hi, qt_lo, 0, 'at_qd')):
                    dstb = buf(dname)
                    for tt in range(NTT):
                        if L == 1 and pair == 0 and m == 'k':
                            pop_ep(4)
                        bk = tt % 2
                        fw = (pbank[bk].w() + wtok[m]
                              + [buf(f'xn_ts{4 * tt + i}').wr for i in range(4)])
                        if tt == 0:
                            fw = fw + dstb.w()
                        for c in range(NCH):
                            tok = mm3(psum[bk][:, :],
                                      w_hi[m][:, c * 128:(c + 1) * 128],
                                      w_lo[m][:, c * 128:(c + 1) * 128],
                                      xn_hi[:, c * T + tt * 512:c * T + tt * 512 + 512],
                                      xn_lo[:, c * T + tt * 512:c * T + tt * 512 + 512],
                                      start=(c == 0), stop=(c == NCH - 1),
                                      wait=fw if c == 0 else None)
                        pbank[bk].setw(tok)
                        buf(f'at_w{m}').addr(tok)
                        sbb = buf(f'at_stag{bk}')
                        bias_ap = bipp[:, L * 16 + bcol + pair:L * 16 + bcol + pair + 1]
                        t_ev = sop(lambda s, p=psum[bk], dd=stag[bk], b=bias_ap:
                                   s.activation(dd, p[:, :], Act.Identity, bias=b),
                                   wait=[tok] + sbb.w())
                        sbb.setw(t_ev)
                        pbank[bk].addr(t_ev)
                        t_sp = split(stag[bk], dhi[:, tt * 512:(tt + 1) * 512],
                                     dlo[:, tt * 512:(tt + 1) * 512],
                                     wait=[t_ev] + (dstb.w() if tt == 0 else []))
                        sbb.addr(t_sp)
                        if tt == NTT - 1:
                            dstb.setw(t_sp)
                        pop_bg(2)
                if L == 0:
                    qdb = buf('at_qpd')
                    for tt in range(NTT):
                        bk = tt % 2
                        fw = pbank[bk].w() + [t_asp, buf('at_qd').wr]
                        if tt == 0:
                            fw = fw + qdb.w()
                        tok = mm3(psum[bk][:, :],
                                  ahi[:, pair * 128:(pair + 1) * 128],
                                  alo[:, pair * 128:(pair + 1) * 128],
                                  qt_hi[:, tt * 512:(tt + 1) * 512],
                                  qt_lo[:, tt * 512:(tt + 1) * 512],
                                  start=True, stop=True, wait=fw)
                        pbank[bk].setw(tok)
                        sbb = buf(f'at_stag{bk}')
                        t_ev = vop(lambda v, p=psum[bk], dd=stag[bk]:
                                   v.tensor_copy(dd, p[:, :]), wait=[tok] + sbb.w())
                        sbb.setw(t_ev)
                        pbank[bk].addr(t_ev)
                        t_sp = split(stag[bk], qp_hi[:, tt * 512:(tt + 1) * 512],
                                     qp_lo[:, tt * 512:(tt + 1) * 512],
                                     wait=[t_ev] + (qdb.w() if tt == 0 else []))
                        sbb.addr(t_sp)
                        if tt == NTT - 1:
                            qdb.setw(t_sp)
                        pop_bg(2)
                    sq_hi, sq_lo = qp_hi, qp_lo
                    qdname = 'at_qpd'
                else:
                    sq_hi, sq_lo = qt_hi, qt_lo
                    qdname = 'at_qd'

                t_bvb = dma('q', bvb,
                            H['bias_row'][0:1, L * 2048 + pair * 128:
                                          L * 2048 + pair * 128 + 128]
                            .partition_broadcast(128),
                            ch='bvb', wait=buf('at_stag1').w())
                buf('at_stag1').setw(t_bvb)
                vb = buf('at_vd')
                rrh = v_hi[:, :].rearrange("p (a b) -> p a b", b=130)
                rrl = v_lo[:, :].rearrange("p (a b) -> p a b", b=130)
                t_ms1 = vop(lambda v: v.memset(rrh[:, :, 64:65], 1.0), wait=vb.w())
                t_ms2 = vop(lambda v: v.memset(rrh[:, :, 129:130], 1.0))
                t_ms3 = vop(lambda v: v.memset(rrl[:, :, 64:65], 0.0))
                t_ms4 = vop(lambda v: v.memset(rrl[:, :, 129:130], 0.0))
                t_ms = [t_ms1, t_ms2, t_ms3, t_ms4]
                t_vlast = None
                for tsub in range(NTS):
                    bk = 2 + tsub % 2
                    for c in range(NCH):
                        tok = mm3(psum[bk][:, 0:128],
                                  xn_hi[:, c * T + tsub * 128:c * T + tsub * 128 + 128],
                                  xn_lo[:, c * T + tsub * 128:c * T + tsub * 128 + 128],
                                  w_hi['v'][:, c * 128:(c + 1) * 128],
                                  w_lo['v'][:, c * 128:(c + 1) * 128],
                                  start=(c == 0), stop=(c == NCH - 1),
                                  wait=(pbank[bk].w() + wtok['v']
                                        + [buf(f'xn_ts{tsub}').wr])
                                  if c == 0 else None)
                    pbank[bk].setw(tok)
                    buf('at_wv').addr(tok)
                    base = tsub * 130
                    vfb = buf('at_stag0')
                    t_f = vop(lambda v, p=psum[bk]: v.tensor_tensor(vf32, p[:, 0:128],
                                                                    bvb[:, :], Alu.add),
                              wait=[tok, t_bvb] + vfb.w() + t_ms)
                    vfb.setw(t_f)
                    pbank[bk].addr(t_f)
                    t_h1 = sop(lambda s, b0=base: s.copy(v_hi[:, b0:b0 + 64],
                                                          vf32[:, 0:64]), wait=t_f)
                    t_h2 = sop(lambda s, b0=base: s.copy(v_hi[:, b0 + 65:b0 + 129],
                                                         vf32[:, 64:128]))
                    t_l1 = vop(lambda v, b0=base: v.tensor_tensor(
                        v_lo[:, b0:b0 + 64], vf32[:, 0:64], v_hi[:, b0:b0 + 64],
                        Alu.subtract), wait=[t_h1, t_h2])
                    t_l2 = vop(lambda v, b0=base: v.tensor_tensor(
                        v_lo[:, b0 + 65:b0 + 129], vf32[:, 64:128],
                        v_hi[:, b0 + 65:b0 + 129], Alu.subtract))
                    vfb.addr(t_l2)
                    t_vlast = t_l2
                    pop_bg(1)
                t_vready = [t_vlast] + t_ms

                SCB = ((0, 1), (5, 6))
                DNB = (4, 7)
                AVB = (2, 3)
                for qt in range(NTT):
                    kts = kts_of(qt)
                    n = len(kts)
                    sp_toks = {0: {}, 1: {}}
                    av_tok = [None, None]
                    dn_tok = [None, None]
                    for i in range(n + 2):
                        if i >= 2:
                            kt = kts[i - 2]
                            j = (i - 2) % 2
                            for hh in range(2):
                                avbank = AVB[hh]
                                av_out = psum[avbank][0:64, :] if hh == 0 else psum[avbank][64:128, :]
                                av_tile = None if hh == 0 else (0, 64)
                                dnb = DNB[hh]
                                v0 = kt * 130 + (0 if hh == 0 else 65)
                                onecol = kt * 130 + (64 if hh == 0 else 129)
                                first = i == 2
                                lastav = i == n + 1
                                w0 = [sp_toks[hh][i - 2]]
                                if first:
                                    w0 = w0 + pbank[avbank].w() + pbank[dnb].w() + t_vready
                                pb.emit('t', lambda t, v0=v0, hh=hh, j=j, o=av_out, tp=av_tile, fz=first:
                                        t.matmul(o, v_hi[:, v0:v0 + 64], p_hi[hh][j][:, :],
                                                 start=fz, stop=False,
                                                 **({} if tp is None else {'tile_position': tp})),
                                        wait=w0)
                                pb.emit('t', lambda t, v0=v0, hh=hh, j=j, o=av_out, tp=av_tile:
                                        t.matmul(o, v_hi[:, v0:v0 + 64], p_lo[hh][j][:, :],
                                                 start=False, stop=False,
                                                 **({} if tp is None else {'tile_position': tp})))
                                av_tok[hh] = pb.emit('t', lambda t, v0=v0, hh=hh, j=j, o=av_out, tp=av_tile, lz=lastav:
                                                     t.matmul(o, v_lo[:, v0:v0 + 64], p_hi[hh][j][:, :],
                                                              start=False, stop=lz,
                                                              **({} if tp is None else {'tile_position': tp})),
                                                     inc=lastav)
                                pb.emit('t', lambda t, oc=onecol, hh=hh, j=j, db=dnb, fz=first:
                                        t.matmul(psum[db][0:1, :], v_hi[:, oc:oc + 1],
                                                 p_hi[hh][j][:, :], start=fz, stop=False))
                                dn_tok[hh] = pb.emit('t', lambda t, oc=onecol, hh=hh, j=j, db=dnb, lz=lastav:
                                                     t.matmul(psum[db][0:1, :], v_hi[:, oc:oc + 1],
                                                              p_lo[hh][j][:, :], start=False, stop=lz),
                                                     inc=lastav)
                                buf(f'at_ph{hh}{j}').addr(dn_tok[hh])
                        if i < n:
                            kt = kts[i]
                            mc = mclass(qt, kt)
                            for hh in range(2):
                                hb = 64 * hh
                                sb_ = SCB[hh][i % 2]
                                w0 = pbank[sb_].w()
                                if i == 0:
                                    w0 = w0 + wtok['k'] + [buf('at_kd').wr, buf(qdname).wr]
                                pb.emit('t', lambda t, b=sb_, kt=kt, hb=hb, qt=qt:
                                        t.matmul(psum[b][:, :],
                                                 k_hi[hb:hb + 64, kt * 128:(kt + 1) * 128],
                                                 sq_hi[hb:hb + 64, qt * 512:(qt + 1) * 512],
                                                 start=True, stop=False), wait=w0)
                                pb.emit('t', lambda t, b=sb_, kt=kt, hb=hb, qt=qt:
                                        t.matmul(psum[b][:, :],
                                                 k_hi[hb:hb + 64, kt * 128:(kt + 1) * 128],
                                                 sq_lo[hb:hb + 64, qt * 512:(qt + 1) * 512],
                                                 start=False, stop=False))
                                last = mc is None
                                t_sc = pb.emit('t', lambda t, b=sb_, kt=kt, hb=hb, qt=qt, lz=last:
                                               t.matmul(psum[b][:, :],
                                                        k_lo[hb:hb + 64, kt * 128:(kt + 1) * 128],
                                                        sq_hi[hb:hb + 64, qt * 512:(qt + 1) * 512],
                                                        start=False, stop=lz), inc=last)
                                if mc is not None:
                                    t_sc = pb.emit('t', lambda t, b=sb_, mc=mc:
                                                   t.matmul(psum[b][:, :], atri[:, :],
                                                            bmask[:, mc:mc + 512],
                                                            start=False, stop=True), inc=True)
                                pbank[sb_].setw(t_sc)
                                pfb = buf(f'at_pf{hh}')
                                t_e = sop(lambda s, b=sb_, hh=hh:
                                          s.activation(pf[hh], psum[b][:, :],
                                                       Act.Exp, scale=SCALE),
                                          wait=[t_sc] + pfb.w())
                                pbank[sb_].addr(t_e)
                                j = i % 2
                                phb = buf(f'at_ph{hh}{j}')
                                t_c = pb.emit('g', lambda g, hh=hh, j=j:
                                              g.tensor_copy(p_hi[hh][j][:, :], pf[hh]),
                                              wait=[t_e] + phb.w()
                                              + (list(p_guard) if pair == 0 and qt == 0
                                                 and i < 4 else []), inc=True)
                                t_s = vop(lambda v, hh=hh, j=j:
                                          v.tensor_tensor(p_lo[hh][j][:, :], pf[hh],
                                                          p_hi[hh][j][:, :], Alu.subtract),
                                          wait=[t_c])
                                pfb.setw(t_s)
                                pfb.addr(t_c)
                                phb.setw(t_s)
                                phb.addr(t_c)
                                sp_toks[hh][i] = t_s
                    for hh in range(2):
                        avbank = AVB[hh]
                        dnb = DNB[hh]
                        pbank[avbank].setw(av_tok[hh])
                        pbank[dnb].setw(dn_tok[hh])
                        rrow = recs_t[0:1, 0:512]
                        rb = buf('at_rec')
                        t_r = vop(lambda v, db=dnb, r=rrow: v.reciprocal(r, psum[db][0:1, :]),
                                  wait=pbank[dnb].w() + rb.w())
                        rb.setw(t_r)
                        pbank[dnb].addr(t_r)
                        btile = None if hh == 0 else (0, 64)
                        bout = psum[dnb][0:64, :] if hh == 0 else psum[dnb][64:128, :]
                        t_bc = mm1(bout, ones2d[0:1, 0:64], rrow, True, True,
                                   wait=pbank[dnb].w() + [t_r], tile=btile)
                        pbank[dnb].setw(t_bc)
                        rb.addr(t_bc)
                        rows = slice(0, 64) if hh == 0 else slice(64, 128)
                        oub = buf(f'at_oun{hh}')
                        t_cp = sop(lambda s, ab=avbank, rw=rows:
                                   s.copy(o_un[rw, :], psum[ab][rw, :]),
                                   wait=pbank[avbank].w() + oub.w() + buf('at_pf0').w())
                        pbank[avbank].addr(t_cp)
                        col = pair * T + qt * 512
                        t_f3 = vop(lambda v, rw=rows, db=dnb:
                                   v.tensor_tensor(o_f32[rw, :], o_un[rw, :],
                                                   psum[db][rw, :], Alu.mult),
                                   wait=[t_cp, t_bc] + buf('at_pf1').w())
                        pbank[dnb].addr(t_f3)
                        t_hi2 = sop(lambda s, rw=rows, c=col:
                                    s.copy(O_hi[rw, c:c + 512], o_f32[rw, :]),
                                    wait=[t_f3] + (list(o_guard) if qt == 0 else []))
                        t_lo2 = vop(lambda v, rw=rows, c=col:
                                    v.tensor_tensor(O_lo[rw, c:c + 512], o_f32[rw, :],
                                                    O_hi[rw, c:c + 512], Alu.subtract),
                                    wait=t_hi2)
                        oub.setw(t_lo2)
                        oub.addr(t_hi2)
                        buf('at_pf0').addr(t_cp)
                        buf('at_pf0').addr(t_lo2)
                        buf('at_pf1').addr(t_f3)
                        buf('at_pf1').addr(t_lo2)
                        buf('at_Ofull').addr(t_lo2)
                t_pd = pb.emit('t', lambda t: t.drain(), inc=True)
                buf('at_pairdone').setw(t_pd)
                for nm in ('at_kd', 'at_qd', 'at_qpd', 'at_vd'):
                    buf(nm).addr(t_pd)

            pb.barrier()
            wo_hi = R2[:, 0:4096].bitcast(BF16)
            wo_lo = R2[:, 4096:8192].bitcast(BF16)
            wost = R2[:, 8192:12288]
            t_wo = []
            wo_prev = [None, None]
            for q in range(4):
                sl = q % 2
                half = wost[:, sl * 2048:(sl + 1) * 2048]
                tl = dma('q' if sl == 0 else 's', half,
                         H[f'wo{L}'][:, q * 2048:(q + 1) * 2048], ch=f'wo{sl}',
                         wait=[wo_prev[sl]])
                t_sp = split(half, wo_hi[:, q * 2048:(q + 1) * 2048],
                             wo_lo[:, q * 2048:(q + 1) * 2048], wait=tl)
                wo_prev[sl] = t_sp
                t_wo.append(t_sp)
            bo_b = R3[:, 0:1024]
            tbe1 = dma('q', bo_b,
                       H['bias_row'][0:1, L * 2048 + 1024:L * 2048 + 2048]
                       .partition_broadcast(128),
                       ch='bob', wait=buf('wo_bob').w())
            tbe2 = tbe1
            buf('wo_bob').setw(tbe1)
            sst = [R3[:, 1024:1536], R3[:, 1536:2048]]
            for ts in range(NTS):
                for dh in range(2):
                    bk = (ts * 2 + dh) % 4
                    for p in range(NP):
                        tok = mm3(psum[bk][:, :],
                                  O_hi[:, p * T + ts * 128:p * T + ts * 128 + 128],
                                  O_lo[:, p * T + ts * 128:p * T + ts * 128 + 128],
                                  wo_hi[:, p * 1024 + dh * 512:p * 1024 + dh * 512 + 512],
                                  wo_lo[:, p * 1024 + dh * 512:p * 1024 + dh * 512 + 512],
                                  start=(p == 0), stop=(p == NP - 1),
                                  wait=(pbank[bk].w() + t_wo) if p == 0 else None)
                    pbank[bk].setw(tok)
                    ib = (ts * 2 + dh) % 2
                    ssb = buf(f'wo_sst{ib}')
                    t_ev = vop(lambda v, b=bk, d=sst[ib], dh=dh:
                               v.tensor_tensor(d, psum[b][:, :],
                                               bo_b[:, dh * 512:(dh + 1) * 512], Alu.add),
                               wait=[tok, tbe1, tbe2] + ssb.w())
                    pbank[bk].addr(t_ev)
                    t_st = dma('s', s_sp[ts * 128:(ts + 1) * 128, dh * 512:(dh + 1) * 512],
                               sst[ib], ch=f'sst{ib}', wait=t_ev)
                    ssb.setw(t_ev)
                    ssb.addr(t_st)

        def phase_moe(routing_only=False, index_only=False):
            sc = R3[:, 0:256]
            msc = R3[:, 256:512]
            msc0 = R3[:, 512:768]
            v4s = [R3[:, 768 + i * 16:784 + i * 16] for i in range(4)]
            wsum = R3[:, 832:848]
            rw = R3[:, 848:864]
            gs = R3[:, 880:944]
            tmp64 = R3[:, 944:1008]
            t2gs = R3[:, 1008:1024]
            mgs = R3[:, 1024:1040]
            thr = R3[:, 1040:1056]
            alw = R3[:, 1056:1120]
            scb = buf('rt_sc')
            t_sg = []
            for ts in range(NTS):
                bk = 5 + ts % 2
                for c in range(NCH):
                    tok = mm3(psum[bk][:, 0:16],
                              xn_hi[:, c * T + ts * 128:c * T + ts * 128 + 128],
                              xn_lo[:, c * T + ts * 128:c * T + ts * 128 + 128],
                              wg_hi[:, c * 16:(c + 1) * 16],
                              wg_lo[:, c * 16:(c + 1) * 16],
                              start=(c == 0), stop=(c == NCH - 1),
                              wait=(pbank[bk].w() + [t_wg, buf('xn_ready').wr])
                              if c == 0 else None)
                pbank[bk].setw(tok)
                t_e = sop(lambda s, b=bk, ts=ts: s.activation(sc[:, ts * 16:(ts + 1) * 16],
                                                              psum[b][:, 0:16], Act.Sigmoid),
                          wait=[tok] + (scb.w() if ts == 0 else []))
                pbank[bk].addr(t_e)
                t_sg.append(t_e)
            scb.setw(t_sg[-1])
            rr4 = sc.rearrange("p (a j) -> p a j", j=4)
            pairs6 = [(0, 1), (0, 2), (0, 3), (1, 2), (1, 3), (2, 3)]
            t_gs = None
            for idx, (a, b_) in enumerate(pairs6):
                t_t = vop(lambda v, a=a, b=b_: v.tensor_tensor(
                    tmp64, rr4[:, :, a:a + 1], rr4[:, :, b:b + 1], Alu.add),
                    wait=t_sg if idx == 0 else [t_gs])
                if idx == 0:
                    t_gs = vop(lambda v: v.tensor_copy(gs, tmp64), wait=t_t)
                else:
                    t_gs = vop(lambda v: v.tensor_tensor(gs, gs, tmp64, Alu.max), wait=t_t)
            rrg = gs.rearrange("p (a g) -> p a g", g=4)
            t_th = None
            for idx, (a, b_) in enumerate(pairs6):
                t_t = vop(lambda v, a=a, b=b_: v.tensor_tensor(
                    mgs, rrg[:, :, a:a + 1], rrg[:, :, b:b + 1], Alu.min),
                    wait=[t_gs] if idx == 0 else [t_th])
                if idx == 0:
                    t_th = vop(lambda v: v.tensor_copy(thr, mgs), wait=t_t)
                else:
                    t_th = vop(lambda v: v.tensor_tensor(thr, thr, mgs, Alu.max), wait=t_t)
            thr_b = thr.rearrange("p (a j) -> p a j", j=1).broadcast_to((128, 16, 4))
            t_al = vop(lambda v: v.tensor_tensor(alw.rearrange("p (a g) -> p a g", g=4),
                                                 rrg, thr_b, Alu.is_ge), wait=[t_th])
            t_am = vop(lambda v: v.tensor_scalar(tmp64, alw, 1.0, -MASKV,
                                                 Alu.subtract, Alu.mult), wait=t_al)
            am_b = tmp64.rearrange("p (a j) -> p a j", j=1).broadcast_to((128, 64, 4))
            t_m0 = vop(lambda v: v.tensor_tensor(msc0.rearrange("p (a j) -> p a j", j=4),
                                                 rr4, am_b, Alu.add), wait=[t_am] + t_sg)
            t_mc = vop(lambda v: v.tensor_copy(msc, msc0), wait=t_m0)
            rr16 = msc.rearrange("p (a j) -> p a j", j=16)
            t_vi = None
            for i in range(4):
                t_vi = vop(lambda v, i=i: v.reduce_max(v4s[i], rr16, axis=Ax.X), wait=t_mc)
                if i < 3:
                    vb_ = v4s[i].rearrange("p (a j) -> p a j", j=1).broadcast_to((128, 16, 16))
                    t_eq = vop(lambda v, vb=vb_: v.tensor_tensor(
                        comb[:, :].rearrange("p (a j) -> p a j", j=16), rr16, vb, Alu.is_ge),
                        wait=t_vi)
                    t_mk = vop(lambda v: v.tensor_scalar_mul(comb[:, :], comb[:, :], MASKV), wait=t_eq)
                    t_mc = vop(lambda v: v.tensor_tensor(msc, msc, comb[:, :], Alu.add), wait=t_mk)
            t_ws = vop(lambda v: v.tensor_tensor(wsum, v4s[0], v4s[1], Alu.add), wait=t_vi)
            t_ws = vop(lambda v: v.tensor_tensor(wsum, wsum, v4s[2], Alu.add), wait=t_ws)
            t_ws = vop(lambda v: v.tensor_tensor(wsum, wsum, v4s[3], Alu.add), wait=t_ws)
            t_ws = vop(lambda v: v.tensor_scalar_add(wsum, wsum, 1e-6), wait=t_ws)
            t_rw = vop(lambda v: v.reciprocal(rw, wsum), wait=t_ws)
            v4b = v4s[3].rearrange("p (a j) -> p a j", j=1).broadcast_to((128, 16, 16))
            t_sel = vop(lambda v: v.tensor_tensor(comb[:, :].rearrange("p (a j) -> p a j", j=16),
                                                  msc0.rearrange("p (a j) -> p a j", j=16),
                                                  v4b, Alu.is_ge), wait=[t_vi, t_mc])
            t_c1 = vop(lambda v: v.tensor_tensor(msc, sc, comb[:, :], Alu.mult), wait=[t_sel])
            rwb = rw.rearrange("p (a j) -> p a j", j=1).broadcast_to((128, 16, 16))
            t_comb = vop(lambda v: v.tensor_tensor(comb[:, :].rearrange("p (a j) -> p a j", j=16),
                                                   msc.rearrange("p (a j) -> p a j", j=16),
                                                   rwb, Alu.mult), wait=[t_c1, t_rw])
            buf('rt_comb').setw(t_comb)
            buf('mx_gs').setw(t_comb)
            buf('mx_hf').setw(t_comb)
            buf('mx_osh').setw(t_comb)
            if routing_only:
                return

            rb_ = buf('or_routed')

            w1_hi = R2[:, 0:2048].bitcast(BF16)
            w1_lo = R2[:, 2048:4096].bitcast(BF16)
            w3_hi = R2[:, 4096:6144].bitcast(BF16)
            w3_lo = R2[:, 6144:8192].bitcast(BF16)
            w2_hi = R2[:, 8192:10240].bitcast(BF16)
            w2_lo = R2[:, 10240:12288].bitcast(BF16)
            hh_hi = R3[:, 1536:2560].bitcast(BF16)
            hh_lo = R3[:, 2560:3584].bitcast(BF16)
            g_s = R3[:, 0:512]
            hf32 = R3[:, 512:1024]
            wmats = (('w1', w1_hi, w1_lo, H['moe_w1']),
                     ('w3', w3_hi, w3_lo, H['moe_w3']),
                     ('w2', w2_hi, w2_lo, H['moe_w2']))

            pstg = [ahi[:, :].bitcast(F32), xn_lo[:, :].bitcast(F32)[:, 0:512]]
            pqh = alo[:, :].bitcast(F32)[:, 0:256].bitcast(BF16)
            pql = alo[:, :].bitcast(F32)[:, 256:512].bitcast(BF16)
            prep_q = []

            def _prep_step(e, mi, nm, hsrc, qtr):
                sl = 0 if e < 2 else qtr % 2
                sb = buf(f'wp_stg{sl}')
                t_l = dma('q', pstg[sl], hsrc[e][:, qtr * 512:(qtr + 1) * 512],
                          ch=f'wpl{sl}', wait=sb.w())
                sb.setw(t_l)
                hb = buf('wp_qh')
                t_h = sop(lambda s, sl=sl: s.copy(pqh, pstg[sl]), wait=[t_l] + hb.w())
                hb.setw(t_h)
                sb.addr(t_h)
                lb = buf('wp_ql')
                t_s = vop(lambda v, sl=sl: v.tensor_tensor(pql, pstg[sl], pqh,
                                                           Alu.subtract),
                          wait=[t_h] + lb.w())
                lb.setw(t_s)
                sb.addr(t_s)
                hb.addr(t_s)
                t_sh = dma('s', wsp_hi[e * 3 + mi][:, qtr * 512:(qtr + 1) * 512],
                           pqh, ch='wps0', wait=[t_h])
                hb.addr(t_sh)
                t_sl = dma('s', wsp_lo[e * 3 + mi][:, qtr * 512:(qtr + 1) * 512],
                           pql, ch='wps1', wait=[t_s])
                lb.addr(t_sl)
                prep_tok[(e, nm)][0].append(t_sh)
                prep_tok[(e, nm)][1].append(t_sl)

            def make_prep_steps(e):
                for mi, (nm, _, _, hsrc) in enumerate(wmats):
                    prep_tok[(e, nm)] = ([], [])
                    for qtr in range(8):
                        prep_q.append(lambda e=e, mi=mi, nm=nm, h=hsrc, q=qtr:
                                      _prep_step(e, mi, nm, h, q))

            def pop_prep(n):
                for _ in range(n):
                    if prep_q:
                        prep_q.pop(0)()

            lw_chain = {'w1': [None, None], 'w3': [None, None]}

            def load_expert_w(ex):
                wrd = {}
                for mi, (nm, whi, wlo, hsrc) in enumerate(wmats):
                    sh_toks, sl_toks = prep_tok[(ex, nm)]
                    if nm == 'w2':
                        wb_ = buf(f'mx_{nm}')
                        t_hi = dma('q', whi[:, :], wsp_hi[ex * 3 + mi],
                                   ch=f'mxh{nm}', wait=wb_.w() + sh_toks)
                        t_lo = dma('s', wlo[:, :], wsp_lo[ex * 3 + mi],
                                   ch=f'mxl{nm}', wait=wb_.w() + sl_toks)
                        wb_.setw([t_hi, t_lo])
                        wrd[nm] = [t_hi, t_lo]
                        continue
                    rh = whi[:, :].rearrange("p (c q) -> p c q", q=512)
                    rl = wlo[:, :].rearrange("p (c q) -> p c q", q=512)
                    sh = wsp_hi[ex * 3 + mi].rearrange("p (c q) -> p c q", q=512)
                    sl = wsp_lo[ex * 3 + mi].rearrange("p (c q) -> p c q", q=512)
                    toks = {}
                    t_hi, t_lo = lw_chain[nm]
                    for fj in range(4):
                        wb_ = buf(f'mx_{nm}_f{fj}')
                        t_hi = dma('q', rh[:, :, fj * 128:(fj + 1) * 128],
                                   sh[:, :, fj * 128:(fj + 1) * 128],
                                   ch=f'mxh{nm}',
                                   wait=wb_.w() + sh_toks + [t_hi])
                        t_lo = dma('s', rl[:, :, fj * 128:(fj + 1) * 128],
                                   sl[:, :, fj * 128:(fj + 1) * 128],
                                   ch=f'mxl{nm}',
                                   wait=wb_.w() + sl_toks + [t_lo])
                        wb_.setw([t_hi, t_lo])
                        toks[fj] = [t_hi, t_lo]
                    lw_chain[nm] = [t_hi, t_lo]
                    wrd[nm] = toks
                return wrd

            selx = OR[:, 15136:15392]
            selT = OR[0:16, 0:2048]
            posx = OR[0:16, 2048:4096]
            posT = OR[:, 4096:4352]
            eq_t = OR[:, 4352:4608]
            ework = OR[:, 4608:4864]
            eid_k = OR[:, 4864:4880]
            posk = OR[:, 4880:4896]
            idxk = OR[:, 4896:4912]
            idxku = [OR[:, 4912 + k * 16:4928 + k * 16].bitcast(mybir.dt.uint32)
                     for k in range(4)]
            eidx_sb = OR[:, 4976:5232]
            tok_sb = OR[:, 5232:5248].bitcast(mybir.dt.uint32)
            idxu = R4[:, 0:96].bitcast(mybir.dt.uint32)
            combg = [R4[:, 96:192], R4[:, 192:288]]
            t_csp = dma('s', comb_h[0:T, :].rearrange("(a p) e -> p a e", p=128),
                        comb[:, :].rearrange("p (a e) -> p a e", e=16),
                        ch='combsp', wait=[buf('rt_comb').wr])
            t_eix = dma('q', eidx_sb, H['eidx'][:, :], ch='mxidx', wait=rb_.w())
            t_tok = dma('q', tok_sb, H['tokids'][:, :], ch='mxidx2', wait=rb_.w())
            t_pf = dma('s', idxl_h[:, :].rearrange("(a b) c -> a (b c)", b=768),
                       H['padfill'][:, :], ch='padf')
            t_selx = vop(lambda v: v.tensor_single_scalar(selx, comb[:, :], 0.0, Alu.is_gt),
                         wait=[buf('rt_comb').wr] + buf('rt_selx').w())
            buf('rt_selx').setw(t_selx)
            ev_toks = []
            for grp in range(4):
                bk = grp % 2
                tp_toks = []
                for k in range(4):
                    ts4 = grp * 4 + k
                    t_tp = pb.emit('t', lambda t, b=bk, ts4=ts4, k=k:
                                   t.transpose(psum[b][0:16, k * 128:(k + 1) * 128],
                                               selx[:, ts4 * 16:(ts4 + 1) * 16],
                                               ident[:, :]),
                                   wait=[t_selx] + (pbank[bk].w() if k == 0 else []),
                                   inc=(k == 3))
                    tp_toks.append(t_tp)
                pbank[bk].setw(tp_toks[-1])
                t_ev = vop(lambda v, b=bk, grp=grp:
                           v.tensor_copy(selT[:, grp * 512:(grp + 1) * 512],
                                         psum[b][0:16, 0:512]),
                           wait=[tp_toks[-1], rb_.wr])
                pbank[bk].addr(t_ev)
                ev_toks.append(t_ev)
            t_scan = vop(lambda v: v.tensor_tensor_scan(posx, selT, selT, 0.0,
                                                        Alu.add, Alu.bypass),
                         wait=ev_toks)
            ptp = []
            for ts in range(NTS):
                bk = ts % 2
                t_tp = pb.emit('t', lambda t, b=bk, ts=ts:
                               t.transpose(psum[b][0:128, 0:16],
                                           posx[:, ts * 128:(ts + 1) * 128],
                                           ident[0:16, 0:16]),
                               wait=[t_scan] + pbank[bk].w(), inc=True)
                pbank[bk].setw(t_tp)
                t_ev = vop(lambda v, b=bk, ts=ts:
                           v.tensor_copy(posT[:, ts * 16:(ts + 1) * 16],
                                         psum[b][0:128, 0:16]), wait=t_tp)
                pbank[bk].addr(t_ev)
                ptp.append(t_ev)
            rr16i = lambda ap: ap.rearrange("p (a j) -> p a j", j=16)
            sc_toks = []
            last_round = []
            for k in range(4):
                vkb = v4s[k].rearrange("p (a j) -> p a j", j=1).broadcast_to((128, 16, 16))
                t_eq = vop(lambda v, vb=vkb: v.tensor_tensor(
                    rr16i(eq_t), rr16i(msc0), vb, Alu.is_equal),
                    wait=[buf('rt_comb').wr] + (ptp if k == 0 else []) + last_round)
                t_em = vop(lambda v: v.tensor_tensor(ework, eq_t, eidx_sb, Alu.mult),
                           wait=[t_eq, t_eix])
                t_ei = vop(lambda v: v.reduce_sum(eid_k, rr16i(ework), axis=Ax.X),
                           wait=t_em)
                t_pk = vop(lambda v: v.tensor_tensor(ework, posT, eq_t, Alu.mult),
                           wait=t_ei)
                t_ps = vop(lambda v: v.reduce_sum(posk, rr16i(ework), axis=Ax.X),
                           wait=t_pk)
                t_ix = vop(lambda v: v.tensor_scalar_mul(idxk, posk, 16.0), wait=t_ps)
                t_i2 = vop(lambda v: v.tensor_tensor(idxk, idxk, eid_k, Alu.add), wait=t_ix)
                t_i3 = vop(lambda v: v.tensor_scalar_add(idxk, idxk, -16.0), wait=t_i2)
                t_u = vop(lambda v, k=k: v.tensor_copy(idxku[k][:, :], idxk), wait=t_i3)
                last_round = [t_u]
                for ts in range(NTS):
                    sc_toks.append(pb.emit(
                        'g', lambda g, k=k, ts=ts: g.indirect_dma_start(
                            out=idxl_h[:, :],
                            out_offset=bass.IndirectOffsetOnAxis(
                                ap=idxku[k][:, ts:ts + 1], axis=0),
                            in_=tok_sb[:, ts:ts + 1], in_offset=None),
                        wait=[t_u, t_tok, t_pf], inc=True, dma=True,
                        ch=f'isc{ts % 4}'))
            for nm in ('mx_gs', 'mx_hf', 'mx_osh'):
                buf(nm).addr(last_round[0])
            if index_only:
                return
            t_idxu = dma('q', idxu[:, :].rearrange("p (b e) -> p b e", e=16),
                         idxl_h[:, :].rearrange("(b p e) c -> p b (e c)", b=6, p=128),
                         ch='idxu', wait=sc_toks)

            rsw_tok = {}
            acc_tok = [None]

            def accum_chain(tok_fn, wait):
                t = tok_fn(([acc_tok[0]] if acc_tok[0] else []) + wait)
                acc_tok[0] = t
                return t

            outsh = R3[:, 1024:1536]
            if (16, 'w1') not in prep_tok:
                make_sprep(16)
                make_sprep(17)
                pop_bg(len(bg_q))
            make_prep_steps(0)
            make_prep_steps(1)
            for sx in range(2):
                ex = 16 + sx
                wrd = load_expert_w(ex)
                hbuf = buf('mx_h')
                for tt in range(NTT):
                    h_toks = []
                    for fj in range(4):
                        bg = fj % 2
                        bu = 2 + fj % 2
                        for c in range(NCH):
                            tokg = mm3(psum[bg][:, :],
                                       w1_hi[:, c * 512 + fj * 128:c * 512 + fj * 128 + 128],
                                       w1_lo[:, c * 512 + fj * 128:c * 512 + fj * 128 + 128],
                                       xn_hi[:, c * T + tt * 512:c * T + tt * 512 + 512],
                                       xn_lo[:, c * T + tt * 512:c * T + tt * 512 + 512],
                                       start=(c == 0), stop=(c == NCH - 1),
                                       wait=(pbank[bg].w() + wrd['w1'][fj])
                                       if c == 0 else None)
                        pbank[bg].setw(tokg)
                        buf(f'mx_w1_f{fj}').addr(tokg)
                        for c in range(NCH):
                            toku = mm3(psum[bu][:, :],
                                       w3_hi[:, c * 512 + fj * 128:c * 512 + fj * 128 + 128],
                                       w3_lo[:, c * 512 + fj * 128:c * 512 + fj * 128 + 128],
                                       xn_hi[:, c * T + tt * 512:c * T + tt * 512 + 512],
                                       xn_lo[:, c * T + tt * 512:c * T + tt * 512 + 512],
                                       start=(c == 0), stop=(c == NCH - 1),
                                       wait=(pbank[bu].w() + wrd['w3'][fj])
                                       if c == 0 else None)
                        pbank[bu].setw(toku)
                        buf(f'mx_w3_f{fj}').addr(toku)
                        gsb = buf('mx_gs')
                        t_gs2 = sop(lambda s, b=bg: s.activation(g_s, psum[b][:, :], Act.Silu),
                                    wait=[tokg] + gsb.w())
                        gsb.setw(t_gs2)
                        pbank[bg].addr(t_gs2)
                        hfb = buf('mx_hf')
                        t_h = vop(lambda v, b=bu: v.tensor_tensor(hf32, psum[b][:, :],
                                                                  g_s, Alu.mult),
                                  wait=[toku, t_gs2] + hfb.w())
                        hfb.setw(t_h)
                        gsb.addr(t_h)
                        pbank[bu].addr(t_h)
                        t_sp = split(hf32, hh_hi[:, fj * 512:(fj + 1) * 512],
                                     hh_lo[:, fj * 512:(fj + 1) * 512],
                                     wait=[t_h] + (hbuf.w() if fj == 0 else []))
                        hfb.addr(t_sp)
                        h_toks.append(t_sp)
                        pop_prep(2)
                    hbuf.setw(h_toks)
                    for tl in range(4):
                        ts = tt * 4 + tl
                        for dh in range(2):
                            bk = 6 + (tl * 2 + dh) % 2
                            for fj in range(4):
                                tok = mm3(psum[bk][:, :],
                                          hh_hi[:, fj * 512 + tl * 128:fj * 512 + tl * 128 + 128],
                                          hh_lo[:, fj * 512 + tl * 128:fj * 512 + tl * 128 + 128],
                                          w2_hi[:, fj * 1024 + dh * 512:fj * 1024 + dh * 512 + 512],
                                          w2_lo[:, fj * 1024 + dh * 512:fj * 1024 + dh * 512 + 512],
                                          start=(fj == 0), stop=(fj == 3),
                                          wait=(pbank[bk].w() + h_toks + wrd['w2'])
                                          if fj == 0 else None)
                            pbank[bk].setw(tok)
                            hbuf.addr(tok)
                            buf('mx_w2').addr(tok)
                            osb = buf('mx_osh')
                            t_ev = vop(lambda v, b=bk: v.tensor_copy(outsh, psum[b][:, :]),
                                       wait=[tok] + osb.w())
                            pbank[bk].addr(t_ev)
                            if sx == 0:
                                t_st = dma('s', routed_h[ts * 128:(ts + 1) * 128,
                                                         dh * 512:(dh + 1) * 512],
                                           outsh, ch=f'rsw{(tl * 2 + dh) % 2}',
                                           wait=t_ev)
                                rsw_tok[(ts, dh)] = t_st
                            else:
                                t_st = accum_chain(
                                    lambda w, ts=ts, dh=dh: pb.emit(
                                        'g', lambda g: g.dma_start(
                                            out=routed_h[ts * 128:(ts + 1) * 128,
                                                         dh * 512:(dh + 1) * 512],
                                            in_=outsh, accum_op=Alu.add),
                                        wait=w, inc=True, dma=True, ch='racc'),
                                    [t_ev, rsw_tok[(ts, dh)]])
                            osb.setw(t_ev)
                            osb.addr(t_st)
            buf('mx_xg1').setw([toku])
            buf('wp_stg1').setw([toku])

            xg_f32 = [OR[:, 0:1024], xn_hi[:, :].bitcast(F32)[:, 0:1024]]
            xng_hi = [OR[:, 1024 + i * 6144:1024 + i * 6144 + 3072].bitcast(BF16)
                      for i in range(2)]
            xng_lo = [OR[:, 4096 + i * 6144:4096 + i * 6144 + 3072].bitcast(BF16)
                      for i in range(2)]
            hs_hi = OR[:, 13312:14848].bitcast(BF16)
            hs_lo = OR[:, 14848:16384].bitcast(BF16)
            gs_s = R3[:, 0:768]
            hfs = R3[:, 768:1536]
            oasm = [R3[:, 1536:2560], R3[:, 2560:3584]]

            pf_q = []

            def prefetch_expert(ex):
                pi = ex % 2
                xb2 = buf(f'mx_xng{pi}')
                cgb = buf(f'mx_cg{pi}')
                sp_toks2 = []
                cg_toks = []
                for b6 in range(6):
                    t_c = pb.emit('g', lambda g, b6=b6, ex=ex, pi=pi:
                                  g.indirect_dma_start(
                                      out=combg[pi][:, b6 * 16:(b6 + 1) * 16],
                                      out_offset=None,
                                      in_=comb_h[:, :],
                                      in_offset=bass.IndirectOffsetOnAxis(
                                          ap=idxu[:, b6 * 16 + ex:b6 * 16 + ex + 1],
                                          axis=0)),
                                  wait=([t_idxu, t_csp] + (cgb.w() if b6 == 0 else [])),
                                  inc=True, dma=True, ch=f'mxcg{pi}')
                    cg_toks.append(t_c)
                cgb.setw(cg_toks)

                def blk(b6):
                    gi = (ex * 6 + b6) % 2
                    xgb = buf(f'mx_xg{gi}')
                    t_g = pb.emit('g', lambda g, b6=b6, ex=ex, gi=gi:
                                  g.indirect_dma_start(
                                      out=xg_f32[gi], out_offset=None,
                                      in_=xn_tm[:, :],
                                      in_offset=bass.IndirectOffsetOnAxis(
                                          ap=idxu[:, b6 * 16 + ex:b6 * 16 + ex + 1],
                                          axis=0)),
                                  wait=[t_idxu] + xgb.w(), inc=True, dma=True,
                                  ch=f'mxg{gi}')
                    xgb.setw(t_g)
                    for c in range(NCH):
                        bk = c % 2
                        t_tp = pb.emit('t', lambda t, b=bk, c=c, gi=gi:
                                       t.transpose(psum[b][0:128, (c % 4) * 128:(c % 4) * 128 + 128],
                                                   xg_f32[gi][:, c * 128:(c + 1) * 128],
                                                   ident[:, :]),
                                       wait=[t_g] + pbank[bk].w(), inc=True)
                        pbank[bk].setw(t_tp)
                        xgb.addr(t_tp)
                        col = c * 768 + b6 * 128
                        t_sp2 = split(psum[bk][0:128, (c % 4) * 128:(c % 4) * 128 + 128],
                                      xng_hi[pi][:, col:col + 128],
                                      xng_lo[pi][:, col:col + 128],
                                      wait=[t_tp] + (xb2.w() if (b6 == 0 and c == 0) else []))
                        pbank[bk].addr(t_sp2)
                        sp_toks2.append(t_sp2)
                    if b6 == 5:
                        xb2.setw(list(sp_toks2))

                for b6 in range(6):
                    pf_q.append(lambda b6=b6: blk(b6))
                return sp_toks2, cg_toks

            def pop_pf(n):
                for _ in range(n):
                    if pf_q:
                        pf_q.pop(0)()

            pre = {0: (None, None)}
            pre[0] = prefetch_expert(0)
            pop_pf(6)
            pop_prep(len(prep_q))
            wrd_next = load_expert_w(0)
            for ex in range(16):
                pi = ex % 2
                wrd = wrd_next
                sp_toks2, cg_toks = pre[ex]
                xb2 = buf(f'mx_xng{pi}')
                cgb = buf(f'mx_cg{pi}')
                if ex + 2 < 16:
                    make_prep_steps(ex + 2)
                if ex + 1 < 16:
                    pre[ex + 1] = prefetch_expert(ex + 1)
                h_toks2 = []
                gsb = buf('mx_gs2')
                hfb = buf('mx_hf2')
                hbuf2 = buf('mx_h2')
                for fj in range(4):
                    for tn, (c0, cw) in enumerate(((0, 512), (512, 256))):
                        bg = 2 + tn
                        bu = 4 + tn
                        for c in range(NCH):
                            tokg = mm3(psum[bg][:, 0:cw],
                                       w1_hi[:, c * 512 + fj * 128:c * 512 + fj * 128 + 128],
                                       w1_lo[:, c * 512 + fj * 128:c * 512 + fj * 128 + 128],
                                       xng_hi[pi][:, c * 768 + c0:c * 768 + c0 + cw],
                                       xng_lo[pi][:, c * 768 + c0:c * 768 + c0 + cw],
                                       start=(c == 0), stop=(c == NCH - 1),
                                       wait=(pbank[bg].w() + wrd['w1'][fj] + sp_toks2)
                                       if c == 0 else None)
                        pbank[bg].setw(tokg)
                        buf(f'mx_w1_f{fj}').addr(tokg)
                        xb2.addr(tokg)
                        for c in range(NCH):
                            toku = mm3(psum[bu][:, 0:cw],
                                       w3_hi[:, c * 512 + fj * 128:c * 512 + fj * 128 + 128],
                                       w3_lo[:, c * 512 + fj * 128:c * 512 + fj * 128 + 128],
                                       xng_hi[pi][:, c * 768 + c0:c * 768 + c0 + cw],
                                       xng_lo[pi][:, c * 768 + c0:c * 768 + c0 + cw],
                                       start=(c == 0), stop=(c == NCH - 1),
                                       wait=(pbank[bu].w() + wrd['w3'][fj])
                                       if c == 0 else None)
                        pbank[bu].setw(toku)
                        buf(f'mx_w3_f{fj}').addr(toku)
                        xb2.addr(toku)
                        t_gs2 = sop(lambda s, b=bg, c0=c0, cw=cw:
                                    s.activation(gs_s[:, c0:c0 + cw], psum[b][:, 0:cw],
                                                 Act.Silu),
                                    wait=[tokg] + gsb.w())
                        gsb.setw(t_gs2)
                        pbank[bg].addr(t_gs2)
                        t_h = vop(lambda v, b=bu, c0=c0, cw=cw:
                                  v.tensor_tensor(hfs[:, c0:c0 + cw], psum[b][:, 0:cw],
                                                  gs_s[:, c0:c0 + cw], Alu.mult),
                                  wait=[toku, t_gs2] + hfb.w()
                                  + (buf('mx_osh').w() if ex == 0 and fj == 0
                                     and tn == 0 else []))
                        hfb.setw(t_h)
                        pbank[bu].addr(t_h)
                        gsb.addr(t_h)
                        col = fj * 768 + c0
                        t_sp3 = split(hfs[:, c0:c0 + cw], hs_hi[:, col:col + cw],
                                      hs_lo[:, col:col + cw],
                                      wait=[t_h] + (hbuf2.w() if fj == 0 and tn == 0 else []))
                        hfb.addr(t_sp3)
                        h_toks2.append(t_sp3)
                        pop_prep(3)
                        pop_pf(1)
                hbuf2.setw(h_toks2)
                for b6 in range(6):
                    ob = b6 % 2
                    oab = buf(f'mx_oa{ob}')
                    ev2 = []
                    for dh in range(2):
                        bk = 6 + dh
                        for fj in range(4):
                            tok = mm3(psum[bk][:, :],
                                      hs_hi[:, fj * 768 + b6 * 128:fj * 768 + b6 * 128 + 128],
                                      hs_lo[:, fj * 768 + b6 * 128:fj * 768 + b6 * 128 + 128],
                                      w2_hi[:, fj * 1024 + dh * 512:fj * 1024 + dh * 512 + 512],
                                      w2_lo[:, fj * 1024 + dh * 512:fj * 1024 + dh * 512 + 512],
                                      start=(fj == 0), stop=(fj == 3),
                                      wait=(pbank[bk].w() + h_toks2 + wrd['w2'])
                                      if fj == 0 else None)
                        pbank[bk].setw(tok)
                        hbuf2.addr(tok)
                        buf('mx_w2').addr(tok)
                        t_ev = vop(lambda v, b=bk, dh=dh, ex=ex, b6=b6, pi=pi, ob=ob:
                                   v.tensor_scalar_mul(oasm[ob][:, dh * 512:(dh + 1) * 512],
                                                       psum[b][:, :],
                                                       combg[pi][:, b6 * 16 + ex:b6 * 16 + ex + 1]),
                                   wait=[tok] + cg_toks + oab.w())
                        pbank[bk].addr(t_ev)
                        cgb.addr(t_ev)
                        ev2.append(t_ev)
                    t_sc2 = accum_chain(
                        lambda w, ex=ex, b6=b6, ob=ob: pb.emit(
                            'g', lambda g: g.indirect_dma_start(
                                out=routed_h[:, :],
                                out_offset=bass.IndirectOffsetOnAxis(
                                    ap=idxu[:, b6 * 16 + ex:b6 * 16 + ex + 1], axis=0),
                                in_=oasm[ob], in_offset=None,
                                compute_op=Alu.add),
                            wait=w, inc=True, dma=True, ch='racc'),
                        ev2 + list(rsw_tok.values()))
                    oab.setw(ev2)
                    oab.addr(t_sc2)
                if ex + 1 < 16:
                    pop_prep(len(prep_q))
                    wrd_next = load_expert_w(ex + 1)
            rb_.setw([acc_tok[0]])

        ep_q = []
        ep_done = []

        def queue_ep0ln_steps():
            r_sb = [OR[:, 0:512], OR[:, 512:1024]]
            s_sb = [OR[:, 1024:1536], OR[:, 1536:2048]]
            x_sb = [OR[:, 2048:2560], OR[:, 2560:3072]]
            x1f = [OR[:, 3072:4096], OR[:, 4096:5120]]
            xnf1_ = [OR[:, 5120:6144], OR[:, 7168:8192]]
            gt_ = [OR[:, 6144:6656], OR[:, 6656:7168]]

            def step(ts):
                ib = ts % 2
                x1b = buf(f'e1_x1{ib}')
                t2s = []
                for dh in range(2):
                    ib2 = (ts * 2 + dh) % 2
                    sbf = buf(f'e1_s{ib2}')
                    t_ls = dma('s', s_sb[ib2],
                               s_sp[ts * 128:(ts + 1) * 128, dh * 512:(dh + 1) * 512],
                               ch=f'eps{ib2}', wait=sbf.w())
                    sbf.setw(t_ls)
                    rbf = buf(f'e1_r{ib2}')
                    t_lr = dma('q', r_sb[ib2],
                               routed_h[ts * 128:(ts + 1) * 128, dh * 512:(dh + 1) * 512],
                               ch=f'e0r{ib2}',
                               wait=rbf.w() + (buf('or_routed').w() if ts == 0 else []))
                    rbf.setw(t_lr)
                    xbf = buf(f'e1_x{ib2}')
                    t_lx = dma('g', x_sb[ib2],
                               H['x_in'][ts * 128:(ts + 1) * 128, dh * 512:(dh + 1) * 512],
                               ch=f'epx{ib2}', wait=xbf.w())
                    xbf.setw(t_lx)
                    gbf = buf(f'e1_g{ib2}')
                    t_g = sop(lambda s, ib2=ib2: s.activation(gt_[ib2], r_sb[ib2],
                                                              Act.Silu),
                              wait=[t_lr] + gbf.w())
                    rbf.addr(t_g)
                    t_1 = vop(lambda v, ib2=ib2, ib=ib, dh=dh:
                              v.tensor_tensor(x1f[ib][:, dh * 512:(dh + 1) * 512],
                                              gt_[ib2], s_sb[ib2], Alu.mult),
                              wait=[t_g, t_ls] + (x1b.w() if dh == 0 else []))
                    gbf.setw(t_g)
                    gbf.addr(t_1)
                    sbf.addr(t_1)
                    t_2 = vop(lambda v, ib2=ib2, ib=ib, dh=dh:
                              v.tensor_tensor(x1f[ib][:, dh * 512:(dh + 1) * 512],
                                              x1f[ib][:, dh * 512:(dh + 1) * 512],
                                              x_sb[ib2], Alu.add), wait=[t_1, t_lx])
                    xbf.addr(t_2)
                    t2s.append(t_2)
                t_o = dma('s', x1_sp[ts * 128:(ts + 1) * 128, :], x1f[ib][:, :],
                          ch=f'epo{ib}', wait=t2s)
                ep_done.append(t_o)
                x1b.setw(t2s)
                x1b.addr(t_o)
                k8 = (ts % 4) * 8
                st = stats[:, k8:k8 + 8]
                xnf1 = xnf1_[ib]
                stb = buf(f'e1_stats{ts % 4}')
                xnb = buf(f'e1_xnf{ib}')
                t_sum = vop(lambda v, ib=ib, st=st: v.reduce_sum(st[:, 0:1], x1f[ib],
                                                                 axis=Ax.X),
                            wait=t2s + stb.w())
                t_sq = sop(lambda s, ib=ib, xnf1=xnf1: s.activation(xnf1, x1f[ib],
                                                                    Act.Square),
                           wait=t2s + xnb.w())
                t_ssq = vop(lambda v, st=st, xnf1=xnf1: v.reduce_sum(st[:, 1:2], xnf1,
                                                                    axis=Ax.X),
                            wait=t_sq)
                t_m = vop(lambda v, st=st: v.tensor_scalar_mul(st[:, 2:3], st[:, 0:1],
                                                               1.0 / D), wait=t_sum)
                t_m2 = vop(lambda v, st=st: v.tensor_tensor(st[:, 3:4], st[:, 2:3],
                                                            st[:, 2:3], Alu.mult),
                           wait=t_m)
                t_v = vop(lambda v, st=st: v.scalar_tensor_tensor(st[:, 4:5], st[:, 1:2],
                                                                  1.0 / D, st[:, 3:4],
                                                                  Alu.mult, Alu.subtract),
                          wait=[t_ssq, t_m2])
                t_sd = sop(lambda s, st=st: s.activation(st[:, 5:6], st[:, 4:5], Act.Sqrt,
                                                         bias=st[:, 7:8]), wait=t_v)
                t_r = vop(lambda v, st=st: v.reciprocal(st[:, 6:7], st[:, 5:6]),
                          wait=t_sd)
                stb.setw(t_r)
                t_xn = vop(lambda v, ib=ib, st=st, xnf1=xnf1:
                           v.tensor_scalar(xnf1, x1f[ib], st[:, 2:3], st[:, 6:7],
                                           Alu.subtract, Alu.mult),
                           wait=[t_r] + xnb.w())
                xnb.setw(t_xn)
                x1b.addr(t_xn)
                stb.addr(t_xn)
                sp_list = []
                for c in range(NCH):
                    bk = 6 + c % 2
                    t_tp = pb.emit('t', lambda t, c=c, b=bk, xnf1=xnf1:
                                   t.transpose(psum[b][:, 0:128],
                                               xnf1[:, c * 128:(c + 1) * 128],
                                               ident[:, :]),
                                   wait=[t_xn] + pbank[bk].w(), inc=True)
                    pbank[bk].setw(t_tp)
                    xnb.addr(t_tp)
                    col = c * T + ts * 128
                    t_sp = split(psum[bk][:, 0:128],
                                 xn_hi[:, col:col + 128], xn_lo[:, col:col + 128],
                                 wait=t_tp)
                    pbank[bk].addr(t_sp)
                    sp_list.append(t_sp)
                buf(f'xn_ts{ts}').setw(sp_list)
                ep_done.append(sp_list[-1])

            for ts in range(NTS):
                ep_q.append(lambda ts=ts: step(ts))

        def pop_ep(n):
            for _ in range(n):
                if ep_q:
                    ep_q.pop(0)()

        def phase_ffn1():
            wgb = [R2[:, 0:512].bitcast(BF16), R2[:, 512:1024].bitcast(BF16)]
            wub = [R2[:, 1024:1536].bitcast(BF16), R2[:, 1536:2048].bitcast(BF16)]
            wdb = [R2[:, 2048:2560].bitcast(BF16), R2[:, 2560:3072].bitcast(BF16)]
            h_hi = OR[:, 0:6144].bitcast(BF16)
            h_lo = OR[:, 6144:12288].bitcast(BF16)
            g_s = R3[:, 0:512]
            hf32 = R3[:, 512:1024]
            s_sb = [R3[:, 1024:1536], R3[:, 1536:2048]]
            x_sb = [R3[:, 2048:2560], R3[:, 2560:3072]]
            gt_ = R3[:, 3072:3584]
            orb = buf('or_routed')
            for tt in range(NTT):
                h_toks = []
                for fc in range(NFC):
                    ib = fc % 2
                    wgbuf = buf(f'f1_wg{ib}')
                    t_lg = dma('g', wgb[ib][:, :], H['wg1'][fc], ch=f'f1g{ib}', wait=wgbuf.w())
                    wgbuf.setw(t_lg)
                    wubuf = buf(f'f1_wu{ib}')
                    t_lu = dma('g', wub[ib][:, :], H['wu1'][fc], ch=f'f1u{ib}', wait=wubuf.w())
                    wubuf.setw(t_lu)
                    bg = fc % 2
                    bu = 2 + fc % 2
                    for c in range(NCH):
                        tokg = mm2(psum[bg][:, :], wgb[ib][:, c * 128:(c + 1) * 128],
                                   xn_hi[:, c * T + tt * 512:c * T + tt * 512 + 512],
                                   xn_lo[:, c * T + tt * 512:c * T + tt * 512 + 512],
                                   start=(c == 0), stop=(c == NCH - 1),
                                   wait=(pbank[bg].w() + [t_lg, buf('xn_ready').wr,
                                                          orb.wr])
                                   if c == 0 else None)
                    pbank[bg].setw(tokg)
                    wgbuf.addr(tokg)
                    for c in range(NCH):
                        toku = mm2(psum[bu][:, :], wub[ib][:, c * 128:(c + 1) * 128],
                                   xn_hi[:, c * T + tt * 512:c * T + tt * 512 + 512],
                                   xn_lo[:, c * T + tt * 512:c * T + tt * 512 + 512],
                                   start=(c == 0), stop=(c == NCH - 1),
                                   wait=(pbank[bu].w() + [t_lu]) if c == 0 else None)
                    pbank[bu].setw(toku)
                    wubuf.addr(toku)
                    gsb = buf('f1_gs')
                    t_g = sop(lambda s, b=bg: s.activation(g_s, psum[b][:, :], Act.Silu,
                                                           scale=1.0 / 127.0),
                              wait=[tokg] + gsb.w())
                    gsb.setw(t_g)
                    pbank[bg].addr(t_g)
                    hfb = buf('f1_hf')
                    t_h = vop(lambda v, b=bu: v.scalar_tensor_tensor(
                        hf32, psum[b][:, :], 1.0 / 127.0, g_s, Alu.mult, Alu.mult),
                        wait=[toku, t_g] + hfb.w())
                    hfb.setw(t_h)
                    gsb.addr(t_h)
                    pbank[bu].addr(t_h)
                    hb2 = buf('f1_h')
                    t_sp = split(hf32, h_hi[:, fc * 512:(fc + 1) * 512],
                                 h_lo[:, fc * 512:(fc + 1) * 512],
                                 wait=[t_h] + (hb2.w() if fc == 0 else []))
                    hfb.addr(t_sp)
                    h_toks.append(t_sp)
                buf('f1_h').setw(h_toks)
                dn_tok = [None] * 8
                for fc in range(NFC):
                    ib = fc % 2
                    wdbuf = buf(f'f1_wd{ib}')
                    t_ld = dma('g', wdb[ib][:, :], H['wd1'][:, fc * 1024:(fc + 1) * 1024],
                               ch=f'f1d{ib}', wait=wdbuf.w())
                    wdbuf.setw(t_ld)
                    for tl in range(4):
                        for dh in range(2):
                            bk = tl * 2 + dh
                            fw = None
                            if fc == 0:
                                fw = pbank[bk].w() + h_toks + [t_ld]
                            elif tl == 0 and dh == 0:
                                fw = [t_ld]
                            tok = mm2L(psum[bk][:, :],
                                       h_hi[:, fc * 512 + tl * 128:fc * 512 + tl * 128 + 128],
                                       h_lo[:, fc * 512 + tl * 128:fc * 512 + tl * 128 + 128],
                                       wdb[ib][:, dh * 512:(dh + 1) * 512],
                                       start=(fc == 0), stop=(fc == NFC - 1), wait=fw)
                            if fc == NFC - 1:
                                dn_tok[bk] = tok
                                pbank[bk].setw(tok)
                            buf('f1_h').addr(tok)
                            wdbuf.addr(tok)
                for tl in range(4):
                    ts = tt * 4 + tl
                    for dh in range(2):
                        bk = tl * 2 + dh
                        ib = (tl * 2 + dh) % 2
                        sbf = buf(f'f1_s{ib}')
                        t_ls = dma('s', s_sb[ib],
                                   s_sp[ts * 128:(ts + 1) * 128, dh * 512:(dh + 1) * 512],
                                   ch=f'f1s{ib}', wait=sbf.w())
                        sbf.setw(t_ls)
                        xbf = buf(f'f1_x{ib}')
                        t_lx = dma('s', x_sb[ib],
                                   x1_sp[ts * 128:(ts + 1) * 128, dh * 512:(dh + 1) * 512],
                                   ch=f'f1x{ib}', wait=xbf.w())
                        xbf.setw(t_lx)
                        gbf = buf('f1_gt')
                        t_g = sop(lambda s, b=bk: s.activation(gt_, psum[b][:, :], Act.Silu,
                                                               scale=1.0 / 127.0),
                                  wait=[dn_tok[bk]] + gbf.w())
                        gbf.setw(t_g)
                        pbank[bk].addr(t_g)
                        hfb = buf('f1_hf')
                        t_1 = vop(lambda v, s_=s_sb[ib]: v.tensor_tensor(hf32, gt_, s_,
                                                                         Alu.mult),
                                  wait=[t_g, t_ls] + hfb.w())
                        gbf.addr(t_1)
                        sbf.addr(t_1)
                        t_2 = vop(lambda v, x=x_sb[ib]: v.tensor_tensor(hf32, hf32, x,
                                                                        Alu.add),
                                  wait=[t_1, t_lx])
                        hfb.setw(t_2)
                        xbf.addr(t_2)
                        t_o = dma('s', out_h[ts * 128:(ts + 1) * 128, dh * 512:(dh + 1) * 512],
                                  hf32, ch=f'f1o{(tl * 2 + dh) % 2}', wait=t_2)
                        hfb.addr(t_o)

        ln0_done = []
        if 'ln0' in phases:
            if 'at0' in phases:
                at_stage_cache[(0, 0)] = stage_attn_w(0, 0)
            ln0_done = phase_ln(0)
            if dbg:
                pb.barrier()
            tap('xnhi0', xn_hi[:, :].bitcast(F32))
            tap('xnlo0', xn_lo[:, :].bitcast(F32))
        if 'at0' in phases:
            if 'moe' in phases:
                make_sprep(16)
                make_sprep(17)
            phase_attn(0, o_guard=ln0_done, p_guard=ln0_done)
            pop_bg(len(bg_q))
            pb.barrier()
            tap('O0', OR[:, :])
            tap('s0', s_sp[:, :])
        if 'moe_rt' in phases:
            phase_moe(routing_only=True)
            pb.barrier()
            tap('rt_r3', R3[:, 0:1120])
        if 'moe' in phases:
            phase_moe()
            pb.barrier()
            tap('routed0', routed_h[0:T, :])
            tap('comb', comb[:, :])
        if 'ep0' in phases:
            queue_ep0ln_steps()
            if 'at1' not in phases:
                pop_ep(len(ep_q))
                pb.barrier()
                tap('xnhi1', xn_hi[:, :].bitcast(F32))
        if 'at1' in phases:
            phase_attn(1, o_guard=ep_done)
            pb.barrier()
        if 'f1' in phases:
            phase_ffn1()
        pb.barrier()

        for chn in pb.chcnt:
            sems[chn] = es.enter_context(nc.semaphore('sem_' + chn))
        with nc.Block() as blk:
            blk.tensor(lambda e: [f(e) for f in pb.fns['t']])
            blk.vector(lambda e: [f(e) for f in pb.fns['v']])
            blk.scalar(lambda e: [f(e) for f in pb.fns['s']])
            blk.gpsimd(lambda e: [f(e) for f in pb.fns['g']])
            blk.sync(lambda e: [f(e) for f in pb.fns['q']])
    return nc, pb


_SHARED_CACHE = {}


def kernel(**inputs):
    host = prep_host(inputs)
    nc, pb = build_program()
    x = np.asarray(inputs['L0_attn_x'])
    shared = {k: v for k, v in host.items() if not k.startswith('_')}
    in_maps = []
    for b in range(8):
        m = dict(shared)
        m['x_in'] = np.ascontiguousarray(x[b])
        in_maps.append(m)
    res = run_bass_kernel_spmd(nc, in_maps, core_ids=list(range(8)))
    out = np.stack([res.results[b]['out'] for b in range(8)], axis=0)
    return out.astype(np.float32)


if __name__ == '__main__':
    nc, pb = build_program()
    print({e: len(pb.fns[e]) for e in ENGS})


# version 19 (final)
# speedup vs baseline: 1.0210x; 1.0210x over previous
import numpy as np
import ml_dtypes
import concourse.bass as bass
import concourse.mybir as mybir
from concourse.bass_utils import run_bass_kernel_spmd

F32 = mybir.dt.float32
BF16 = mybir.dt.bfloat16
I32 = mybir.dt.int32
Alu = mybir.AluOpType
Act = mybir.ActivationFunctionType
Ax = mybir.AxisListType
BF = ml_dtypes.bfloat16

D = 1024
T = 2048
NCH = 8
NTS = 16
NTT = 4
NH = 16
HD = 64
NP = 8
SCALE = HD ** -0.5
NEX = 18
NFC = 24
MASKV = -1e30

ENGS = ('t', 'v', 's', 'g', 'q')


class PB:
    def __init__(self, nc):
        self.nc = nc
        self.fns = {e: [] for e in ENGS}
        self.cnt = {k: 0 for k in ENGS}
        self.chcnt = {}
        self.waited = {e: {} for e in ENGS}
        self.pos = {e: 0 for e in ENGS}
        self.dr = {e: 0 for e in ENGS}
        self.sems = None

    def _flat(self, tok):
        if tok is None:
            return
        if isinstance(tok, tuple):
            yield tok
            return
        for t in tok:
            yield from self._flat(t)

    def wait(self, e, tok):
        need = {}
        for sname, c, src, spos in self._flat(tok):
            if src == e:
                if e == 't':
                    continue
                if self.dr[e] < spos:
                    self.dr[e] = self.pos[e]
                    self.pos[e] += 1
                    self.fns[e].append(lambda eng: eng.drain())
            else:
                if self.waited[e].get(sname, 0) >= c:
                    continue
                if need.get(sname, 0) < c:
                    need[sname] = c
        for sname, c in need.items():
            self.waited[e][sname] = c
            self.fns[e].append(
                lambda eng, s=sname, cc=c: eng.wait_ge(self.sems[s], cc))

    def emit(self, e, fn, wait=None, inc=False, dma=False, ch=None):
        if wait is not None:
            self.wait(e, wait)
        self.pos[e] += 1
        mypos = self.pos[e]
        if inc:
            if dma:
                sname = 'D_' + ch
                step = 16
                self.chcnt[sname] = self.chcnt.get(sname, 0) + step
                c = self.chcnt[sname]
            else:
                sname = e
                step = 1
                self.cnt[sname] += step
                c = self.cnt[sname]
            self.fns[e].append(
                lambda eng, f=fn, s=sname, st=step: f(eng).then_inc(self.sems[s], st))
            return (sname, c, sname if dma else e, mypos)
        self.fns[e].append(fn)
        return None

    def barrier(self, skip_dma=()):
        toks = []
        for e in ENGS:
            toks.append(self.emit(e, lambda eng: eng.drain(), inc=True))
        for s, c in self.chcnt.items():
            if c and s not in skip_dma:
                toks.append((s, c, s, 0))
        for e in ENGS:
            self.wait(e, toks)


class B:
    __slots__ = ('wr', 'rd')

    def __init__(self):
        self.wr = None
        self.rd = []

    def w(self):
        out = [t for t in self.rd if t is not None]
        if self.wr is not None:
            out.append(self.wr)
        return out

    def setw(self, tok):
        self.wr = tok
        self.rd = []

    def addr(self, tok):
        if tok is not None:
            self.rd.append(tok)


def _pairs_layout(w):
    return np.ascontiguousarray(
        w.T.reshape(8, 128, 8, 128).transpose(2, 1, 0, 3).reshape(8, 128, 1024))


def prep_host(ins):
    d = {}
    for L, pre in ((0, 'L0_attn_'), (1, 'L1_attn_')):
        d[f'wq{L}'] = _pairs_layout(np.asarray(ins[pre + 'Wq']))
        d[f'wk{L}'] = _pairs_layout(np.asarray(ins[pre + 'Wk']))
        d[f'wv{L}'] = _pairs_layout(np.asarray(ins[pre + 'Wv']))
        d[f'wo{L}'] = np.ascontiguousarray(
            np.asarray(ins[pre + 'Wo']).T.reshape(8, 128, 1024)
            .transpose(1, 0, 2).reshape(128, 8192))
    bias_pp = np.zeros((128, 32), np.float32)
    bias_row = np.zeros((1, 4096), np.float32)
    for L, pre in ((0, 'L0_attn_'), (1, 'L1_attn_')):
        bias_pp[:, L * 16:L * 16 + 8] = np.asarray(ins[pre + 'bq']).reshape(8, 128).T
        bias_pp[:, L * 16 + 8:L * 16 + 16] = np.asarray(ins[pre + 'bk']).reshape(8, 128).T
        bias_row[0, L * 2048:L * 2048 + 1024] = np.asarray(ins[pre + 'bv'])
        bias_row[0, L * 2048 + 1024:L * 2048 + 2048] = np.asarray(ins[pre + 'bo'])
    d['bias_pp'] = bias_pp
    d['bias_row'] = bias_row

    wc = np.asarray(ins['L0_attn_W_cross'])
    sw = np.asarray(ins['L0_attn_shift_weights'])
    tau = np.arange(HD)[None, :] - np.arange(HD)[:, None]
    valid = (tau >= 0) & (tau < 32)
    m_all = np.where(valid[None], sw[:, np.clip(tau, 0, 31)], np.float32(0.0)).astype(np.float32)
    wcb = np.zeros((128, 1024), np.float32)
    mbk = np.zeros((128, 1024), np.float32)
    for h in range(16):
        pr, half = h // 2, h % 2
        r0, c0 = half * 64, pr * 128 + half * 64
        wcb[r0:r0 + 64, c0:c0 + 64] = wc[h]
        mbk[r0:r0 + 64, c0:c0 + 64] = m_all[h]
    d['wcb'] = wcb
    d['mblk'] = mbk
    u16 = np.zeros((16, 128), np.float32)
    e5 = np.zeros((16, 1024), np.float32)
    for h in range(16):
        u16[h, :] = ((np.arange(128) >= 64) == (h % 2)).astype(np.float32)
        e5[h, (h // 2) * 128:(h // 2) * 128 + 128] = 1.0
    d['u16'] = u16
    d['e5'] = e5
    d['gate'] = np.asarray(ins['L0_attn_gate']).reshape(16, 1).astype(np.float32)

    d['ident'] = np.eye(128, dtype=np.float32)
    d['atri'] = np.asarray(np.triu(np.ones((128, 128), np.float32)), dtype=BF)
    ii = np.arange(128)[:, None]
    cc = np.arange(896)[None, :]
    m = np.where(ii > cc - 384, MASKV, 0.0)
    bm0 = np.zeros((128, 896), np.float32)
    bm0[0] = m[0]
    bm0[1:] = m[1:] - m[:-1]
    d['bm0'] = np.asarray(bm0, dtype=BF)
    cc = np.arange(1152)[None, :]
    m = np.where(np.abs(ii - cc + 512) > 3, MASKV, 0.0)
    bm1 = np.zeros((128, 1152), np.float32)
    bm1[0] = m[0]
    bm1[1:] = m[1:] - m[:-1]
    d['bm1'] = np.asarray(bm1, dtype=BF)
    d['ones_row'] = np.ones((1, 128), np.float32)
    d['ones2d'] = np.ones((128, 64), np.float32)
    eidx = np.zeros((128, 256), np.float32)
    for e in range(16):
        eidx[:, e::16] = e
    d['eidx'] = eidx
    d['tokids'] = (np.arange(16)[None, :] * 128 +
                   np.arange(128)[:, None]).astype(np.uint32)
    d['padfill'] = (2048 + np.arange(12288, dtype=np.uint32) // 16).reshape(16, 768)

    w1 = np.asarray(ins['L0_ffn_w1_e'])
    w3 = np.asarray(ins['L0_ffn_w3_e'])
    w2 = np.asarray(ins['L0_ffn_w2_e'])
    w1s = np.asarray(ins['L0_ffn_w1_s'])
    w3s = np.asarray(ins['L0_ffn_w3_s'])
    w2s = np.asarray(ins['L0_ffn_w2_s'])

    def exp_in(w):
        return np.ascontiguousarray(
            w.T.reshape(8, 128, 512).transpose(1, 0, 2).reshape(128, 4096))

    def exp_out(w):
        return np.ascontiguousarray(
            w.T.reshape(4, 128, 1024).transpose(1, 0, 2).reshape(128, 4096))

    wa = np.zeros((NEX, 128, 4096), np.float32)
    wb = np.zeros((NEX, 128, 4096), np.float32)
    wc2 = np.zeros((NEX, 128, 4096), np.float32)
    for e in range(16):
        wa[e] = exp_in(w1[e])
        wb[e] = exp_in(w3[e])
        wc2[e] = exp_out(w2[e])
    for hf in range(2):
        wa[16 + hf] = exp_in(w1s[hf * 512:(hf + 1) * 512, :])
        wb[16 + hf] = exp_in(w3s[hf * 512:(hf + 1) * 512, :])
        wc2[16 + hf] = exp_out(w2s[:, hf * 512:(hf + 1) * 512])
    d['moe_w1'] = wa
    d['moe_w3'] = wb
    d['moe_w2'] = wc2
    d['wg'] = np.ascontiguousarray(
        np.asarray(ins['L0_ffn_w_gate']).T.reshape(8, 128, 16)
        .transpose(1, 0, 2).reshape(128, 128))

    d['wg1'] = np.ascontiguousarray(np.asarray(ins['L1_ffn_w_gate'])
                                    .T.reshape(8, 128, 24, 128).transpose(2, 1, 0, 3)
                                    .reshape(24, 128, 1024))
    d['wu1'] = np.ascontiguousarray(np.asarray(ins['L1_ffn_w_up'])
                                    .T.reshape(8, 128, 24, 128).transpose(2, 1, 0, 3)
                                    .reshape(24, 128, 1024))
    d['wd1'] = np.ascontiguousarray(np.asarray(ins['L1_ffn_w_down'])
                                    .T.reshape(24, 128, 1024).transpose(1, 0, 2)
                                    .reshape(128, 24576))
    return d


def build_program(dbg=(), phases=('ln0', 'at0', 'moe', 'ep0', 'ln1', 'at1', 'f1')):
    nc = bass.Bass("TRN2", target_bir_lowering=False)
    pb = PB(nc)
    H = {}

    def din(name, shape, dt):
        H[name] = nc.dram_tensor(name, shape, dt, kind="ExternalInput").ap()

    din('x_in', [T, D], F32)
    for L in (0, 1):
        for w in ('wq', 'wk', 'wv'):
            din(f'{w}{L}', [8, 128, 1024], F32)
        din(f'wo{L}', [128, 8192], F32)
    din('bias_pp', [128, 32], F32)
    din('bias_row', [1, 4096], F32)
    din('wcb', [128, 1024], F32)
    din('mblk', [128, 1024], F32)
    din('u16', [16, 128], F32)
    din('e5', [16, 1024], F32)
    din('gate', [16, 1], F32)
    din('ident', [128, 128], F32)
    din('atri', [128, 128], BF16)
    din('bm0', [128, 896], BF16)
    din('bm1', [128, 1152], BF16)
    din('ones_row', [1, 128], F32)
    din('ones2d', [128, 64], F32)
    din('moe_w1', [NEX, 128, 4096], F32)
    din('moe_w3', [NEX, 128, 4096], F32)
    din('moe_w2', [NEX, 128, 4096], F32)
    din('wg', [128, 128], F32)
    din('eidx', [128, 256], F32)
    din('tokids', [128, 16], mybir.dt.uint32)
    din('padfill', [16, 768], mybir.dt.uint32)
    din('wg1', [24, 128, 1024], I32)
    din('wu1', [24, 128, 1024], I32)
    din('wd1', [128, 24576], I32)

    out_h = nc.dram_tensor('out', [T, D], F32, kind="ExternalOutput").ap()
    xn_tm = nc.dram_tensor('xn_tm', [T + 768, D], F32, kind="Internal").ap()
    comb_h = nc.dram_tensor('comb_h', [T + 768, 16], F32, kind="Internal").ap()
    routed_h = nc.dram_tensor('routed_h', [T + 768, D], F32, kind="Internal").ap()
    idxl_h = nc.dram_tensor('idxl_h', [12288, 1], mybir.dt.uint32, kind="Internal").ap()
    x1_sp = nc.dram_tensor('x1_spill', [T, D], F32, kind="Internal").ap()
    s_sp = nc.dram_tensor('s_spill', [T, D], F32, kind="Internal").ap()
    wsp_hi = nc.dram_tensor('wsp_hi', [54, 128, 4096], BF16, kind="Internal").ap()
    wsp_lo = nc.dram_tensor('wsp_lo', [54, 128, 4096], BF16, kind="Internal").ap()
    dbg_h = {}
    for name, shape in dbg:
        dbg_h[name] = nc.dram_tensor('dbg_' + name, list(shape), F32,
                                     kind="ExternalOutput").ap()

    from contextlib import ExitStack
    with ExitStack() as es:
        def sbt(name, shape, dt):
            return es.enter_context(nc.sbuf_tensor(name, shape, dt))

        xn_hi = sbt('xn_hi', [128, NCH * T], BF16)
        xn_lo = sbt('xn_lo', [128, NCH * T], BF16)
        OR = sbt('OR', [128, 16384], F32)
        R2 = sbt('R2', [128, 12832], F32)
        R3 = sbt('R3', [128, 3584], F32)
        R4 = sbt('R4', [128, 672], F32)
        ident = sbt('c_ident', [128, 128], F32)
        atri = sbt('c_atri', [128, 128], BF16)
        bm0 = sbt('c_bm0', [128, 896], BF16)
        bm1 = sbt('c_bm1', [128, 1152], BF16)
        ones2d = sbt('c_ones2d', [128, 64], F32)
        bipp = sbt('c_bipp', [128, 32], F32)
        ahi = sbt('a_hi', [128, 1024], BF16)
        alo = sbt('a_lo', [128, 1024], BF16)
        u16 = R4[0:16, 288:416]
        gt16 = R4[0:16, 416:417]
        e5g = None
        adt = None
        wg_hi = sbt('wg_hi', [128, 128], BF16)
        wg_lo = sbt('wg_lo', [128, 128], BF16)
        comb = sbt('comb', [128, 256], F32)
        stats = sbt('stats', [128, 32], F32)

        psum = [es.enter_context(nc.psum_tensor(f'ps{i}', [128, 512], F32))
                for i in range(8)]
        pbank = [B() for _ in range(8)]

        sems = {}
        for sname in ENGS:
            sems[sname] = es.enter_context(nc.semaphore('sem_' + sname))
        pb.sems = sems

        bufs = {}

        def buf(name):
            if name not in bufs:
                bufs[name] = B()
            return bufs[name]

        def dma(qe, out_ap, in_ap, ch='misc', wait=None):
            return pb.emit(qe, lambda e, o=out_ap, i=in_ap: e.dma_start(out=o, in_=i),
                           wait=wait, inc=True, dma=True, ch=ch)

        def vop(fn, wait=None):
            return pb.emit('v', fn, wait=wait, inc=True)

        def sop(fn, wait=None):
            return pb.emit('s', fn, wait=wait, inc=True)

        def split(src_ap, hi_ap, lo_ap, wait=None):
            t1 = sop(lambda s, s_=src_ap, h=hi_ap: s.copy(h, s_), wait=wait)
            return vop(lambda v, s_=src_ap, h=hi_ap, l=lo_ap:
                       v.tensor_tensor(l, s_, h, Alu.subtract), wait=t1)

        def mm3(out_ap, lh, ll, rh, rl, start, stop, wait=None, tile=None):
            kw = {} if tile is None else {'tile_position': tile}
            pb.emit('t', lambda t, o=out_ap, a=lh, b=rh, k=kw:
                    t.matmul(o, a, b, start=start, stop=False, **k), wait=wait)
            pb.emit('t', lambda t, o=out_ap, a=lh, b=rl, k=kw:
                    t.matmul(o, a, b, start=False, stop=False, **k))
            return pb.emit('t', lambda t, o=out_ap, a=ll, b=rh, k=kw:
                           t.matmul(o, a, b, start=False, stop=stop, **k), inc=True)

        def mm2(out_ap, lhs, rh, rl, start, stop, wait=None):
            pb.emit('t', lambda t, o=out_ap, a=lhs, b=rh:
                    t.matmul(o, a, b, start=start, stop=False), wait=wait)
            return pb.emit('t', lambda t, o=out_ap, a=lhs, b=rl:
                           t.matmul(o, a, b, start=False, stop=stop), inc=True)

        def mm2L(out_ap, lh, ll, rhs, start, stop, wait=None):
            pb.emit('t', lambda t, o=out_ap, a=lh, b=rhs:
                    t.matmul(o, a, b, start=start, stop=False), wait=wait)
            return pb.emit('t', lambda t, o=out_ap, a=ll, b=rhs:
                           t.matmul(o, a, b, start=False, stop=stop), inc=True)

        def mm1(out_ap, lh, rh, start, stop, wait=None, tile=None):
            kw = {} if tile is None else {'tile_position': tile}
            return pb.emit('t', lambda t, o=out_ap, a=lh, b=rh, k=kw:
                           t.matmul(o, a, b, start=start, stop=stop, **k),
                           wait=wait, inc=True)

        def tap(name, src_ap):
            if name in dbg_h:
                return dma('g', dbg_h[name][:, :], src_ap, ch='tap')
            return None

        ep_st = {}
        prep_tok = {}
        bg_q = []
        sstg = [R4[:, 288:416], R4[:, 416:544]]
        sqh = R4[:, 544:608].bitcast(BF16)
        sql = R4[:, 608:672].bitcast(BF16)

        def _sprep_step(e, mi, nm, hsrc, ck):
            sl = ck % 2
            sb = buf(f'sp_stg{sl}')
            t_l = dma('q', sstg[sl], hsrc[e][:, ck * 128:(ck + 1) * 128],
                      ch=f'wspl{sl}', wait=sb.w())
            sb.setw(t_l)
            hb = buf('sp_qh')
            t_h = sop(lambda s, sl=sl: s.copy(sqh, sstg[sl]), wait=[t_l] + hb.w())
            hb.setw(t_h)
            sb.addr(t_h)
            lb = buf('sp_ql')
            t_s = vop(lambda v, sl=sl: v.tensor_tensor(sql, sstg[sl], sqh,
                                                       Alu.subtract),
                      wait=[t_h] + lb.w())
            lb.setw(t_s)
            sb.addr(t_s)
            hb.addr(t_s)
            t_sh = dma('s', wsp_hi[e * 3 + mi][:, ck * 128:(ck + 1) * 128], sqh,
                       ch='wsps0', wait=[t_h])
            hb.addr(t_sh)
            t_sl2 = dma('g', wsp_lo[e * 3 + mi][:, ck * 128:(ck + 1) * 128], sql,
                        ch='wsps1', wait=[t_s])
            lb.addr(t_sl2)
            prep_tok[(e, nm)][0].append(t_sh)
            prep_tok[(e, nm)][1].append(t_sl2)

        def make_sprep(e):
            for mi, nm in enumerate(('w1', 'w3', 'w2')):
                prep_tok[(e, nm)] = ([], [])
                for ck in range(32):
                    bg_q.append(lambda e=e, mi=mi, nm=nm, ck=ck:
                                _sprep_step(e, mi, nm, H['moe_' + nm], ck))

        def pop_bg(n):
            for _ in range(n):
                if bg_q:
                    bg_q.pop(0)()

        e5g = R3[0:16, 2048:3072]
        adt = R2[:, 128:1152]
        ct = []
        ci = [0]

        def cld(dst, src, eng='q'):
            ct.append(dma(eng, dst, src, ch=f'const{ci[0] % 4}'))
            ci[0] += 1
        for n, dst in (('ident', ident[:, :]), ('atri', atri[:, :]), ('bm0', bm0[:, :]),
                       ('bm1', bm1[:, :]), ('bias_pp', bipp[:, :]),
                       ('u16', u16[:, :]), ('gate', gt16[:, :])):
            cld(dst, H[n][:, :], 'q' if ci[0] % 2 == 0 else 's')
        cld(ones2d[:, :], H['ones2d'][:, :], 's')
        wcb_s = R3[:, 0:1024]
        mblk_s = R3[:, 1024:2048]
        cld(wcb_s, H['wcb'][:, :], 'q')
        cld(mblk_s, H['mblk'][:, :], 's')
        wgs = R2[:, 0:128]
        cld(wgs, H['wg'][:, :], 'q')
        t_eps = [vop(lambda v, k=k: v.memset(stats[:, k * 8 + 7:k * 8 + 8], 1e-5))
                 for k in range(4)]
        cld(e5g, H['e5'][:, :], 's')
        t_sig = sop(lambda s: s.activation(gt16[:, 0:1], gt16[:, 0:1], Act.Sigmoid), wait=ct)
        t_e5g = vop(lambda v: v.tensor_scalar_mul(e5g, e5g, gt16[:, 0:1]),
                    wait=[t_sig] + ct)
        tg1 = mm1(psum[0][:, :], u16[:, :], e5g[:, 0:512], True, True, wait=[t_e5g] + ct)
        tg2 = mm1(psum[1][:, :], u16[:, :], e5g[:, 512:1024], True, True)
        t_d = vop(lambda v: v.tensor_tensor(adt, wcb_s, mblk_s, Alu.subtract), wait=ct)
        t_a1 = vop(lambda v: v.tensor_tensor(R3[:, 2048:2560], psum[0][:, :],
                                             adt[:, 0:512], Alu.mult), wait=[t_d, tg1, tg2])
        t_a2 = vop(lambda v: v.tensor_tensor(R3[:, 2560:3072], psum[1][:, :],
                                             adt[:, 512:1024], Alu.mult), wait=tg2)
        t_a3 = vop(lambda v: v.tensor_tensor(adt, R3[:, 2048:3072], mblk_s, Alu.add),
                   wait=[t_a1, t_a2])
        t_asp = split(adt, ahi[:, :], alo[:, :], wait=t_a3)
        t_wg = split(wgs, wg_hi[:, :], wg_lo[:, :], wait=ct)
        t_zp = vop(lambda v: v.memset(OR[:, 0:1024], 0.0))
        zp_toks = []
        for i in range(6):
            zp_toks.append(dma('s', xn_tm[T + i * 128:T + (i + 1) * 128, :],
                               OR[:, 0:1024], ch='zpad', wait=[t_zp] + zp_toks))
            zp_toks.append(dma('s', comb_h[T + i * 128:T + (i + 1) * 128, :],
                               OR[:, 0:16], ch='zpad', wait=[t_zp] + zp_toks))
        buf('or_routed').setw(zp_toks)
        pbank[0].setw(tg1); pbank[0].addr(t_a1)
        pbank[1].setw(tg2); pbank[1].addr(t_a2)
        pb.barrier()

        def phase_ln(L):
            src = H['x_in'] if L == 0 else x1_sp
            xt = [R3[:, 0:1024], R3[:, 1024:2048]]
            xnf_ = [R3[:, 2048:3072], OR[:, 0:1024]]
            ln_done = []
            for ts in range(NTS):
                ib = ts % 2
                k8 = (ts % 4) * 8
                st = stats[:, k8:k8 + 8]
                xnf = xnf_[ib]
                xb = buf(f'ln_x{ib}')
                tok_l = dma('s', xt[ib], src[ts * 128:(ts + 1) * 128, :], ch=f'lnx{ib}',
                            wait=xb.w())
                xb.setw(tok_l)
                stb = buf(f'ln_stats{ts % 4}')
                xnb = buf(f'ln_xnf{ib}')
                t_sum = vop(lambda v, x=xt[ib], st=st: v.reduce_sum(st[:, 0:1], x,
                                                                   axis=Ax.X),
                            wait=[tok_l] + stb.w())
                t_sq = sop(lambda s, x=xt[ib], xnf=xnf: s.activation(xnf, x, Act.Square),
                           wait=[tok_l] + xnb.w())
                t_ssq = vop(lambda v, st=st, xnf=xnf: v.reduce_sum(st[:, 1:2], xnf,
                                                                  axis=Ax.X), wait=t_sq)
                t_m = vop(lambda v, st=st: v.tensor_scalar_mul(st[:, 2:3], st[:, 0:1],
                                                               1.0 / D),
                          wait=t_sum)
                t_m2 = vop(lambda v, st=st: v.tensor_tensor(st[:, 3:4], st[:, 2:3],
                                                            st[:, 2:3], Alu.mult),
                           wait=t_m)
                t_v = vop(lambda v, st=st: v.scalar_tensor_tensor(st[:, 4:5], st[:, 1:2],
                                                                  1.0 / D, st[:, 3:4],
                                                                  Alu.mult, Alu.subtract),
                          wait=[t_ssq, t_m2])
                t_sd = sop(lambda s, st=st: s.activation(st[:, 5:6], st[:, 4:5], Act.Sqrt,
                                                         bias=st[:, 7:8]), wait=t_v)
                t_r = vop(lambda v, st=st: v.reciprocal(st[:, 6:7], st[:, 5:6]), wait=t_sd)
                stb.setw(t_r)
                t_xn = vop(lambda v, x=xt[ib], st=st, xnf=xnf:
                           v.tensor_scalar(xnf, x, st[:, 2:3], st[:, 6:7],
                                           Alu.subtract, Alu.mult),
                           wait=[t_r, tok_l] + xnb.w())
                xnb.setw(t_xn)
                xb.addr(t_xn)
                stb.addr(t_xn)
                if L == 0:
                    t_sp0 = dma('s', xn_tm[ts * 128:(ts + 1) * 128, :], xnf,
                                ch=f'xnsp{ib}', wait=t_xn)
                    xnb.addr(t_sp0)
                    ln_done.append(t_sp0)
                sp_list = []
                for c in range(NCH):
                    bk = c % 2
                    t_tp = pb.emit('t', lambda t, c=c, b=bk, xnf=xnf:
                                   t.transpose(psum[b][:, 0:128],
                                               xnf[:, c * 128:(c + 1) * 128], ident[:, :]),
                                   wait=[t_xn] + pbank[bk].w(), inc=True)
                    pbank[bk].setw(t_tp)
                    xnb.addr(t_tp)
                    col = c * T + ts * 128
                    t_sp = split(psum[bk][:, 0:128],
                                 xn_hi[:, col:col + 128], xn_lo[:, col:col + 128],
                                 wait=t_tp)
                    pbank[bk].addr(t_sp)
                    sp_list.append(t_sp)
                buf(f'xn_ts{ts}').setw(sp_list)
            buf('xn_ready').setw(t_sp)
            ln_done.append(t_sp)
            return ln_done

        at_stage_cache = {}

        def stage_attn_w(L, pair):
            wstage_ = R2[:, 9248:9760]
            wtok = {}
            for i, (m, hk) in enumerate((('q', 'wq'), ('k', 'wk'), ('v', 'wv'))):
                whi_ = R2[:, 9760 + i * 512:10272 + i * 512].bitcast(BF16)
                wlo_ = R2[:, 11296 + i * 512:11808 + i * 512].bitcast(BF16)
                hsrc = H[f'{hk}{L}']
                tsl = []
                for hf in range(2):
                    wsb = buf('at_wstage')
                    tl = dma('q', wstage_, hsrc[pair][:, hf * 512:(hf + 1) * 512],
                             ch='atw',
                             wait=wsb.w() + (buf(f'at_w{m}').w() if hf == 0 else []))
                    wsb.setw(tl)
                    tsp = split(wstage_, whi_[:, hf * 512:(hf + 1) * 512],
                                wlo_[:, hf * 512:(hf + 1) * 512], wait=tl)
                    wsb.addr(tsp)
                    tsl.append(tsp)
                buf(f'at_w{m}').setw(tsl)
                wtok[m] = tsl
            return wtok

        def phase_attn(L, o_guard=(), p_guard=()):
            stag = [R2[:, 0:512], R2[:, 512:1024]]
            qt_hi = R2[:, 1024:2048].bitcast(BF16)
            qt_lo = R2[:, 2048:3072].bitcast(BF16)
            k_hi = R2[:, 3072:4096].bitcast(BF16)
            k_lo = R2[:, 4096:5120].bitcast(BF16)
            qp_hi = R2[:, 5120:6144].bitcast(BF16)
            qp_lo = R2[:, 6144:7168].bitcast(BF16)
            v_hi = R2[:, 7168:8208].bitcast(BF16)
            v_lo = R2[:, 8208:9248].bitcast(BF16)
            wstage = R2[:, 9248:9760]
            w_hi = {}
            w_lo = {}
            for i, m in enumerate(('q', 'k', 'v')):
                w_hi[m] = R2[:, 9760 + i * 512:10272 + i * 512].bitcast(BF16)
                w_lo[m] = R2[:, 11296 + i * 512:11808 + i * 512].bitcast(BF16)
            vf32 = R2[:, 0:128]
            bvb = R2[:, 896:1024]
            O_hi = OR[:, 0:8192].bitcast(BF16)
            O_lo = OR[:, 8192:16384].bitcast(BF16)
            pf = [R3[:, 0:512], R3[:, 512:1024]]
            p_hi = [[R3[:, 1024 + h * 1024 + j * 512:1024 + h * 1024 + j * 512 + 256]
                     .bitcast(BF16) for j in range(2)] for h in range(2)]
            p_lo = [[R3[:, 1280 + h * 1024 + j * 512:1280 + h * 1024 + j * 512 + 256]
                     .bitcast(BF16) for j in range(2)] for h in range(2)]
            recs_t = R3[:, 3072:3584]
            o_un = R3[:, 0:512]
            o_f32 = R3[:, 512:1024]
            bmask = bm0 if L == 0 else bm1
            wqh, wkh, wvh = H[f'wq{L}'], H[f'wk{L}'], H[f'wv{L}']

            def kts_of(qt):
                if L == 0:
                    return list(range(0, 4 * qt + 4))
                return list(range(max(0, 4 * qt - 1), min(16, 4 * qt + 5)))

            def mclass(qt, kt):
                if L == 0:
                    dd = 4 * qt - kt
                    return None if dd >= 1 else (3 + dd) * 128
                return (5 - (kt - 4 * qt + 1)) * 128

            for pair in range(NP):
                pd_prev = buf('at_pairdone').wr
                wtok = at_stage_cache.pop((L, pair), None)
                if wtok is None:
                    wtok = stage_attn_w(L, pair)
                pop_bg(6)
                for m, dhi, dlo, bcol, dname in (('k', k_hi, k_lo, 8, 'at_kd'),
                                                 ('q', qt_hi, qt_lo, 0, 'at_qd')):
                    dstb = buf(dname)
                    for tt in range(NTT):
                        if L == 1 and pair == 0 and m == 'k':
                            pop_ep(4)
                        bk = tt % 2
                        fw = (pbank[bk].w() + wtok[m]
                              + [buf(f'xn_ts{4 * tt + i}').wr for i in range(4)])
                        if tt == 0:
                            fw = fw + dstb.w()
                        for c in range(NCH):
                            tok = mm3(psum[bk][:, :],
                                      w_hi[m][:, c * 128:(c + 1) * 128],
                                      w_lo[m][:, c * 128:(c + 1) * 128],
                                      xn_hi[:, c * T + tt * 512:c * T + tt * 512 + 512],
                                      xn_lo[:, c * T + tt * 512:c * T + tt * 512 + 512],
                                      start=(c == 0), stop=(c == NCH - 1),
                                      wait=fw if c == 0 else None)
                        pbank[bk].setw(tok)
                        buf(f'at_w{m}').addr(tok)
                        sbb = buf(f'at_stag{bk}')
                        bias_ap = bipp[:, L * 16 + bcol + pair:L * 16 + bcol + pair + 1]
                        t_ev = sop(lambda s, p=psum[bk], dd=stag[bk], b=bias_ap:
                                   s.activation(dd, p[:, :], Act.Identity, bias=b),
                                   wait=[tok] + sbb.w())
                        sbb.setw(t_ev)
                        pbank[bk].addr(t_ev)
                        t_sp = split(stag[bk], dhi[:, tt * 512:(tt + 1) * 512],
                                     dlo[:, tt * 512:(tt + 1) * 512],
                                     wait=[t_ev] + (dstb.w() if tt == 0 else []))
                        sbb.addr(t_sp)
                        if tt == NTT - 1:
                            dstb.setw(t_sp)
                        pop_bg(2)
                if L == 0:
                    qdb = buf('at_qpd')
                    for tt in range(NTT):
                        bk = tt % 2
                        fw = pbank[bk].w() + [t_asp, buf('at_qd').wr]
                        if tt == 0:
                            fw = fw + qdb.w()
                        tok = mm3(psum[bk][:, :],
                                  ahi[:, pair * 128:(pair + 1) * 128],
                                  alo[:, pair * 128:(pair + 1) * 128],
                                  qt_hi[:, tt * 512:(tt + 1) * 512],
                                  qt_lo[:, tt * 512:(tt + 1) * 512],
                                  start=True, stop=True, wait=fw)
                        pbank[bk].setw(tok)
                        sbb = buf(f'at_stag{bk}')
                        t_ev = vop(lambda v, p=psum[bk], dd=stag[bk]:
                                   v.tensor_copy(dd, p[:, :]), wait=[tok] + sbb.w())
                        sbb.setw(t_ev)
                        pbank[bk].addr(t_ev)
                        t_sp = split(stag[bk], qp_hi[:, tt * 512:(tt + 1) * 512],
                                     qp_lo[:, tt * 512:(tt + 1) * 512],
                                     wait=[t_ev] + (qdb.w() if tt == 0 else []))
                        sbb.addr(t_sp)
                        if tt == NTT - 1:
                            qdb.setw(t_sp)
                        pop_bg(2)
                    sq_hi, sq_lo = qp_hi, qp_lo
                    qdname = 'at_qpd'
                else:
                    sq_hi, sq_lo = qt_hi, qt_lo
                    qdname = 'at_qd'

                t_bvb = dma('q', bvb,
                            H['bias_row'][0:1, L * 2048 + pair * 128:
                                          L * 2048 + pair * 128 + 128]
                            .partition_broadcast(128),
                            ch='bvb', wait=buf('at_stag1').w())
                buf('at_stag1').setw(t_bvb)
                vb = buf('at_vd')
                rrh = v_hi[:, :].rearrange("p (a b) -> p a b", b=130)
                rrl = v_lo[:, :].rearrange("p (a b) -> p a b", b=130)
                t_ms1 = vop(lambda v: v.memset(rrh[:, :, 64:65], 1.0), wait=vb.w())
                t_ms2 = vop(lambda v: v.memset(rrh[:, :, 129:130], 1.0))
                t_ms3 = vop(lambda v: v.memset(rrl[:, :, 64:65], 0.0))
                t_ms4 = vop(lambda v: v.memset(rrl[:, :, 129:130], 0.0))
                t_ms = [t_ms1, t_ms2, t_ms3, t_ms4]
                t_vlast = None
                for tsub in range(NTS):
                    bk = 2 + tsub % 2
                    for c in range(NCH):
                        tok = mm3(psum[bk][:, 0:128],
                                  xn_hi[:, c * T + tsub * 128:c * T + tsub * 128 + 128],
                                  xn_lo[:, c * T + tsub * 128:c * T + tsub * 128 + 128],
                                  w_hi['v'][:, c * 128:(c + 1) * 128],
                                  w_lo['v'][:, c * 128:(c + 1) * 128],
                                  start=(c == 0), stop=(c == NCH - 1),
                                  wait=(pbank[bk].w() + wtok['v']
                                        + [buf(f'xn_ts{tsub}').wr])
                                  if c == 0 else None)
                    pbank[bk].setw(tok)
                    buf('at_wv').addr(tok)
                    base = tsub * 130
                    vfb = buf('at_stag0')
                    t_f = vop(lambda v, p=psum[bk]: v.tensor_tensor(vf32, p[:, 0:128],
                                                                    bvb[:, :], Alu.add),
                              wait=[tok, t_bvb] + vfb.w() + t_ms)
                    vfb.setw(t_f)
                    pbank[bk].addr(t_f)
                    t_h1 = sop(lambda s, b0=base: s.copy(v_hi[:, b0:b0 + 64],
                                                          vf32[:, 0:64]), wait=t_f)
                    t_h2 = sop(lambda s, b0=base: s.copy(v_hi[:, b0 + 65:b0 + 129],
                                                         vf32[:, 64:128]))
                    t_l1 = vop(lambda v, b0=base: v.tensor_tensor(
                        v_lo[:, b0:b0 + 64], vf32[:, 0:64], v_hi[:, b0:b0 + 64],
                        Alu.subtract), wait=[t_h1, t_h2])
                    t_l2 = vop(lambda v, b0=base: v.tensor_tensor(
                        v_lo[:, b0 + 65:b0 + 129], vf32[:, 64:128],
                        v_hi[:, b0 + 65:b0 + 129], Alu.subtract))
                    vfb.addr(t_l2)
                    t_vlast = t_l2
                    pop_bg(1)
                t_vready = [t_vlast] + t_ms

                SCB = ((0, 1), (5, 6))
                DNB = (4, 7)
                AVB = (2, 3)
                for qt in range(NTT):
                    kts = kts_of(qt)
                    n = len(kts)
                    sp_toks = {0: {}, 1: {}}
                    av_tok = [None, None]
                    dn_tok = [None, None]
                    for i in range(n + 2):
                        if i >= 2:
                            kt = kts[i - 2]
                            j = (i - 2) % 2
                            for hh in range(2):
                                avbank = AVB[hh]
                                av_out = psum[avbank][0:64, :] if hh == 0 else psum[avbank][64:128, :]
                                av_tile = None if hh == 0 else (0, 64)
                                dnb = DNB[hh]
                                v0 = kt * 130 + (0 if hh == 0 else 65)
                                onecol = kt * 130 + (64 if hh == 0 else 129)
                                first = i == 2
                                lastav = i == n + 1
                                w0 = [sp_toks[hh][i - 2]]
                                if first:
                                    w0 = w0 + pbank[avbank].w() + pbank[dnb].w() + t_vready
                                pb.emit('t', lambda t, v0=v0, hh=hh, j=j, o=av_out, tp=av_tile, fz=first:
                                        t.matmul(o, v_hi[:, v0:v0 + 64], p_hi[hh][j][:, :],
                                                 start=fz, stop=False,
                                                 **({} if tp is None else {'tile_position': tp})),
                                        wait=w0)
                                pb.emit('t', lambda t, v0=v0, hh=hh, j=j, o=av_out, tp=av_tile:
                                        t.matmul(o, v_hi[:, v0:v0 + 64], p_lo[hh][j][:, :],
                                                 start=False, stop=False,
                                                 **({} if tp is None else {'tile_position': tp})))
                                av_tok[hh] = pb.emit('t', lambda t, v0=v0, hh=hh, j=j, o=av_out, tp=av_tile, lz=lastav:
                                                     t.matmul(o, v_lo[:, v0:v0 + 64], p_hi[hh][j][:, :],
                                                              start=False, stop=lz,
                                                              **({} if tp is None else {'tile_position': tp})),
                                                     inc=lastav)
                                pb.emit('t', lambda t, oc=onecol, hh=hh, j=j, db=dnb, fz=first:
                                        t.matmul(psum[db][0:1, :], v_hi[:, oc:oc + 1],
                                                 p_hi[hh][j][:, :], start=fz, stop=False))
                                dn_tok[hh] = pb.emit('t', lambda t, oc=onecol, hh=hh, j=j, db=dnb, lz=lastav:
                                                     t.matmul(psum[db][0:1, :], v_hi[:, oc:oc + 1],
                                                              p_lo[hh][j][:, :], start=False, stop=lz),
                                                     inc=lastav)
                                buf(f'at_ph{hh}{j}').addr(dn_tok[hh])
                        if i < n:
                            kt = kts[i]
                            mc = mclass(qt, kt)
                            for hh in range(2):
                                hb = 64 * hh
                                sb_ = SCB[hh][i % 2]
                                w0 = pbank[sb_].w()
                                if i == 0:
                                    w0 = w0 + wtok['k'] + [buf('at_kd').wr, buf(qdname).wr]
                                pb.emit('t', lambda t, b=sb_, kt=kt, hb=hb, qt=qt:
                                        t.matmul(psum[b][:, :],
                                                 k_hi[hb:hb + 64, kt * 128:(kt + 1) * 128],
                                                 sq_hi[hb:hb + 64, qt * 512:(qt + 1) * 512],
                                                 start=True, stop=False), wait=w0)
                                pb.emit('t', lambda t, b=sb_, kt=kt, hb=hb, qt=qt:
                                        t.matmul(psum[b][:, :],
                                                 k_hi[hb:hb + 64, kt * 128:(kt + 1) * 128],
                                                 sq_lo[hb:hb + 64, qt * 512:(qt + 1) * 512],
                                                 start=False, stop=False))
                                last = mc is None
                                t_sc = pb.emit('t', lambda t, b=sb_, kt=kt, hb=hb, qt=qt, lz=last:
                                               t.matmul(psum[b][:, :],
                                                        k_lo[hb:hb + 64, kt * 128:(kt + 1) * 128],
                                                        sq_hi[hb:hb + 64, qt * 512:(qt + 1) * 512],
                                                        start=False, stop=lz), inc=last)
                                if mc is not None:
                                    t_sc = pb.emit('t', lambda t, b=sb_, mc=mc:
                                                   t.matmul(psum[b][:, :], atri[:, :],
                                                            bmask[:, mc:mc + 512],
                                                            start=False, stop=True), inc=True)
                                pbank[sb_].setw(t_sc)
                                pfb = buf(f'at_pf{hh}')
                                t_e = sop(lambda s, b=sb_, hh=hh:
                                          s.activation(pf[hh], psum[b][:, :],
                                                       Act.Exp, scale=SCALE),
                                          wait=[t_sc] + pfb.w())
                                pbank[sb_].addr(t_e)
                                j = i % 2
                                phb = buf(f'at_ph{hh}{j}')
                                t_c = pb.emit('g', lambda g, hh=hh, j=j:
                                              g.tensor_copy(p_hi[hh][j][:, :], pf[hh]),
                                              wait=[t_e] + phb.w()
                                              + (list(p_guard) if pair == 0 and qt == 0
                                                 and i < 4 else []), inc=True)
                                t_s = vop(lambda v, hh=hh, j=j:
                                          v.tensor_tensor(p_lo[hh][j][:, :], pf[hh],
                                                          p_hi[hh][j][:, :], Alu.subtract),
                                          wait=[t_c])
                                pfb.setw(t_s)
                                pfb.addr(t_c)
                                phb.setw(t_s)
                                phb.addr(t_c)
                                sp_toks[hh][i] = t_s
                    for hh in range(2):
                        avbank = AVB[hh]
                        dnb = DNB[hh]
                        pbank[avbank].setw(av_tok[hh])
                        pbank[dnb].setw(dn_tok[hh])
                        rrow = recs_t[0:1, 0:512]
                        rb = buf('at_rec')
                        t_r = vop(lambda v, db=dnb, r=rrow: v.reciprocal(r, psum[db][0:1, :]),
                                  wait=pbank[dnb].w() + rb.w())
                        rb.setw(t_r)
                        pbank[dnb].addr(t_r)
                        btile = None if hh == 0 else (0, 64)
                        bout = psum[dnb][0:64, :] if hh == 0 else psum[dnb][64:128, :]
                        t_bc = mm1(bout, ones2d[0:1, 0:64], rrow, True, True,
                                   wait=pbank[dnb].w() + [t_r], tile=btile)
                        pbank[dnb].setw(t_bc)
                        rb.addr(t_bc)
                        rows = slice(0, 64) if hh == 0 else slice(64, 128)
                        oub = buf(f'at_oun{hh}')
                        t_cp = sop(lambda s, ab=avbank, rw=rows:
                                   s.copy(o_un[rw, :], psum[ab][rw, :]),
                                   wait=pbank[avbank].w() + oub.w() + buf('at_pf0').w())
                        pbank[avbank].addr(t_cp)
                        col = pair * T + qt * 512
                        t_f3 = vop(lambda v, rw=rows, db=dnb:
                                   v.tensor_tensor(o_f32[rw, :], o_un[rw, :],
                                                   psum[db][rw, :], Alu.mult),
                                   wait=[t_cp, t_bc] + buf('at_pf1').w())
                        pbank[dnb].addr(t_f3)
                        t_hi2 = sop(lambda s, rw=rows, c=col:
                                    s.copy(O_hi[rw, c:c + 512], o_f32[rw, :]),
                                    wait=[t_f3] + (list(o_guard) if qt == 0 else []))
                        t_lo2 = vop(lambda v, rw=rows, c=col:
                                    v.tensor_tensor(O_lo[rw, c:c + 512], o_f32[rw, :],
                                                    O_hi[rw, c:c + 512], Alu.subtract),
                                    wait=t_hi2)
                        oub.setw(t_lo2)
                        oub.addr(t_hi2)
                        buf('at_pf0').addr(t_cp)
                        buf('at_pf0').addr(t_lo2)
                        buf('at_pf1').addr(t_f3)
                        buf('at_pf1').addr(t_lo2)
                        buf('at_Ofull').addr(t_lo2)
                t_pd = pb.emit('t', lambda t: t.drain(), inc=True)
                buf('at_pairdone').setw(t_pd)
                for nm in ('at_kd', 'at_qd', 'at_qpd', 'at_vd'):
                    buf(nm).addr(t_pd)

            pb.barrier()
            wo_hi = R2[:, 0:4096].bitcast(BF16)
            wo_lo = R2[:, 4096:8192].bitcast(BF16)
            wost = R2[:, 8192:12288]
            t_wo = []
            wo_prev = [None, None]
            for q in range(4):
                sl = q % 2
                half = wost[:, sl * 2048:(sl + 1) * 2048]
                tl = dma('q' if sl == 0 else 's', half,
                         H[f'wo{L}'][:, q * 2048:(q + 1) * 2048], ch=f'wo{sl}',
                         wait=[wo_prev[sl]])
                t_sp = split(half, wo_hi[:, q * 2048:(q + 1) * 2048],
                             wo_lo[:, q * 2048:(q + 1) * 2048], wait=tl)
                wo_prev[sl] = t_sp
                t_wo.append(t_sp)
            bo_b = R3[:, 0:1024]
            tbe1 = dma('q', bo_b,
                       H['bias_row'][0:1, L * 2048 + 1024:L * 2048 + 2048]
                       .partition_broadcast(128),
                       ch='bob', wait=buf('wo_bob').w())
            tbe2 = tbe1
            buf('wo_bob').setw(tbe1)
            sst = [R3[:, 1024:1536], R3[:, 1536:2048]]
            for ts in range(NTS):
                for dh in range(2):
                    bk = (ts * 2 + dh) % 4
                    for p in range(NP):
                        tok = mm3(psum[bk][:, :],
                                  O_hi[:, p * T + ts * 128:p * T + ts * 128 + 128],
                                  O_lo[:, p * T + ts * 128:p * T + ts * 128 + 128],
                                  wo_hi[:, p * 1024 + dh * 512:p * 1024 + dh * 512 + 512],
                                  wo_lo[:, p * 1024 + dh * 512:p * 1024 + dh * 512 + 512],
                                  start=(p == 0), stop=(p == NP - 1),
                                  wait=(pbank[bk].w() + t_wo) if p == 0 else None)
                    pbank[bk].setw(tok)
                    ib = (ts * 2 + dh) % 2
                    ssb = buf(f'wo_sst{ib}')
                    t_ev = vop(lambda v, b=bk, d=sst[ib], dh=dh:
                               v.tensor_tensor(d, psum[b][:, :],
                                               bo_b[:, dh * 512:(dh + 1) * 512], Alu.add),
                               wait=[tok, tbe1, tbe2] + ssb.w())
                    pbank[bk].addr(t_ev)
                    t_st = dma('s', s_sp[ts * 128:(ts + 1) * 128, dh * 512:(dh + 1) * 512],
                               sst[ib], ch=f'sst{ib}', wait=t_ev)
                    ssb.setw(t_ev)
                    ssb.addr(t_st)

        def phase_moe(routing_only=False, index_only=False):
            sc = R3[:, 0:256]
            msc = R3[:, 256:512]
            msc0 = R3[:, 512:768]
            v4s = [R3[:, 768 + i * 16:784 + i * 16] for i in range(4)]
            wsum = R3[:, 832:848]
            rw = R3[:, 848:864]
            gs = R3[:, 880:944]
            tmp64 = R3[:, 944:1008]
            t2gs = R3[:, 1008:1024]
            mgs = R3[:, 1024:1040]
            thr = R3[:, 1040:1056]
            alw = R3[:, 1056:1120]
            scb = buf('rt_sc')
            t_sg = []
            for ts in range(NTS):
                bk = 5 + ts % 2
                for c in range(NCH):
                    tok = mm3(psum[bk][:, 0:16],
                              xn_hi[:, c * T + ts * 128:c * T + ts * 128 + 128],
                              xn_lo[:, c * T + ts * 128:c * T + ts * 128 + 128],
                              wg_hi[:, c * 16:(c + 1) * 16],
                              wg_lo[:, c * 16:(c + 1) * 16],
                              start=(c == 0), stop=(c == NCH - 1),
                              wait=(pbank[bk].w() + [t_wg, buf('xn_ready').wr])
                              if c == 0 else None)
                pbank[bk].setw(tok)
                t_e = sop(lambda s, b=bk, ts=ts: s.activation(sc[:, ts * 16:(ts + 1) * 16],
                                                              psum[b][:, 0:16], Act.Sigmoid),
                          wait=[tok] + (scb.w() if ts == 0 else []))
                pbank[bk].addr(t_e)
                t_sg.append(t_e)
            scb.setw(t_sg[-1])
            rr4 = sc.rearrange("p (a j) -> p a j", j=4)
            pairs6 = [(0, 1), (0, 2), (0, 3), (1, 2), (1, 3), (2, 3)]
            t_gs = None
            for idx, (a, b_) in enumerate(pairs6):
                t_t = vop(lambda v, a=a, b=b_: v.tensor_tensor(
                    tmp64, rr4[:, :, a:a + 1], rr4[:, :, b:b + 1], Alu.add),
                    wait=t_sg if idx == 0 else [t_gs])
                if idx == 0:
                    t_gs = vop(lambda v: v.tensor_copy(gs, tmp64), wait=t_t)
                else:
                    t_gs = vop(lambda v: v.tensor_tensor(gs, gs, tmp64, Alu.max), wait=t_t)
            rrg = gs.rearrange("p (a g) -> p a g", g=4)
            t_th = None
            for idx, (a, b_) in enumerate(pairs6):
                t_t = vop(lambda v, a=a, b=b_: v.tensor_tensor(
                    mgs, rrg[:, :, a:a + 1], rrg[:, :, b:b + 1], Alu.min),
                    wait=[t_gs] if idx == 0 else [t_th])
                if idx == 0:
                    t_th = vop(lambda v: v.tensor_copy(thr, mgs), wait=t_t)
                else:
                    t_th = vop(lambda v: v.tensor_tensor(thr, thr, mgs, Alu.max), wait=t_t)
            thr_b = thr.rearrange("p (a j) -> p a j", j=1).broadcast_to((128, 16, 4))
            t_al = vop(lambda v: v.tensor_tensor(alw.rearrange("p (a g) -> p a g", g=4),
                                                 rrg, thr_b, Alu.is_ge), wait=[t_th])
            t_am = vop(lambda v: v.tensor_scalar(tmp64, alw, 1.0, -MASKV,
                                                 Alu.subtract, Alu.mult), wait=t_al)
            am_b = tmp64.rearrange("p (a j) -> p a j", j=1).broadcast_to((128, 64, 4))
            t_m0 = vop(lambda v: v.tensor_tensor(msc0.rearrange("p (a j) -> p a j", j=4),
                                                 rr4, am_b, Alu.add), wait=[t_am] + t_sg)
            t_mc = vop(lambda v: v.tensor_copy(msc, msc0), wait=t_m0)
            rr16 = msc.rearrange("p (a j) -> p a j", j=16)
            t_vi = None
            for i in range(4):
                t_vi = vop(lambda v, i=i: v.reduce_max(v4s[i], rr16, axis=Ax.X), wait=t_mc)
                if i < 3:
                    vb_ = v4s[i].rearrange("p (a j) -> p a j", j=1).broadcast_to((128, 16, 16))
                    t_eq = vop(lambda v, vb=vb_: v.tensor_tensor(
                        comb[:, :].rearrange("p (a j) -> p a j", j=16), rr16, vb, Alu.is_ge),
                        wait=t_vi)
                    t_mk = vop(lambda v: v.tensor_scalar_mul(comb[:, :], comb[:, :], MASKV), wait=t_eq)
                    t_mc = vop(lambda v: v.tensor_tensor(msc, msc, comb[:, :], Alu.add), wait=t_mk)
            t_ws = vop(lambda v: v.tensor_tensor(wsum, v4s[0], v4s[1], Alu.add), wait=t_vi)
            t_ws = vop(lambda v: v.tensor_tensor(wsum, wsum, v4s[2], Alu.add), wait=t_ws)
            t_ws = vop(lambda v: v.tensor_tensor(wsum, wsum, v4s[3], Alu.add), wait=t_ws)
            t_ws = vop(lambda v: v.tensor_scalar_add(wsum, wsum, 1e-6), wait=t_ws)
            t_rw = vop(lambda v: v.reciprocal(rw, wsum), wait=t_ws)
            v4b = v4s[3].rearrange("p (a j) -> p a j", j=1).broadcast_to((128, 16, 16))
            t_sel = vop(lambda v: v.tensor_tensor(comb[:, :].rearrange("p (a j) -> p a j", j=16),
                                                  msc0.rearrange("p (a j) -> p a j", j=16),
                                                  v4b, Alu.is_ge), wait=[t_vi, t_mc])
            t_c1 = vop(lambda v: v.tensor_tensor(msc, sc, comb[:, :], Alu.mult), wait=[t_sel])
            rwb = rw.rearrange("p (a j) -> p a j", j=1).broadcast_to((128, 16, 16))
            t_comb = vop(lambda v: v.tensor_tensor(comb[:, :].rearrange("p (a j) -> p a j", j=16),
                                                   msc.rearrange("p (a j) -> p a j", j=16),
                                                   rwb, Alu.mult), wait=[t_c1, t_rw])
            buf('rt_comb').setw(t_comb)
            buf('mx_gs').setw(t_comb)
            buf('mx_hf').setw(t_comb)
            buf('mx_osh').setw(t_comb)
            if routing_only:
                return

            rb_ = buf('or_routed')

            w1_hi = R2[:, 0:2048].bitcast(BF16)
            w1_lo = R2[:, 2048:4096].bitcast(BF16)
            w3_hi = R2[:, 4096:6144].bitcast(BF16)
            w3_lo = R2[:, 6144:8192].bitcast(BF16)
            w2_hi = R2[:, 8192:10240].bitcast(BF16)
            w2_lo = R2[:, 10240:12288].bitcast(BF16)
            hh_hi = R3[:, 1536:2560].bitcast(BF16)
            hh_lo = R3[:, 2560:3584].bitcast(BF16)
            g_s = R3[:, 0:512]
            hf32 = R3[:, 512:1024]
            wmats = (('w1', w1_hi, w1_lo, H['moe_w1']),
                     ('w3', w3_hi, w3_lo, H['moe_w3']),
                     ('w2', w2_hi, w2_lo, H['moe_w2']))

            pstg = [ahi[:, :].bitcast(F32), xn_lo[:, :].bitcast(F32)[:, 0:512]]
            pqh = alo[:, :].bitcast(F32)[:, 0:256].bitcast(BF16)
            pql = alo[:, :].bitcast(F32)[:, 256:512].bitcast(BF16)
            prep_q = []

            def _prep_step(e, mi, nm, hsrc, qtr):
                sl = 0 if e < 2 else qtr % 2
                sb = buf(f'wp_stg{sl}')
                t_l = dma('q', pstg[sl], hsrc[e][:, qtr * 512:(qtr + 1) * 512],
                          ch=f'wpl{sl}', wait=sb.w())
                sb.setw(t_l)
                hb = buf('wp_qh')
                t_h = sop(lambda s, sl=sl: s.copy(pqh, pstg[sl]), wait=[t_l] + hb.w())
                hb.setw(t_h)
                sb.addr(t_h)
                lb = buf('wp_ql')
                t_s = vop(lambda v, sl=sl: v.tensor_tensor(pql, pstg[sl], pqh,
                                                           Alu.subtract),
                          wait=[t_h] + lb.w())
                lb.setw(t_s)
                sb.addr(t_s)
                hb.addr(t_s)
                t_sh = dma('s', wsp_hi[e * 3 + mi][:, qtr * 512:(qtr + 1) * 512],
                           pqh, ch='wps0', wait=[t_h])
                hb.addr(t_sh)
                t_sl = dma('s', wsp_lo[e * 3 + mi][:, qtr * 512:(qtr + 1) * 512],
                           pql, ch='wps1', wait=[t_s])
                lb.addr(t_sl)
                prep_tok[(e, nm)][0].append(t_sh)
                prep_tok[(e, nm)][1].append(t_sl)

            def make_prep_steps(e):
                for mi, (nm, _, _, hsrc) in enumerate(wmats):
                    prep_tok[(e, nm)] = ([], [])
                    for qtr in range(8):
                        prep_q.append(lambda e=e, mi=mi, nm=nm, h=hsrc, q=qtr:
                                      _prep_step(e, mi, nm, h, q))

            def pop_prep(n):
                for _ in range(n):
                    if prep_q:
                        prep_q.pop(0)()

            lw_chain = {'w1': [None, None], 'w3': [None, None]}

            def load_expert_w(ex):
                wrd = {}
                for mi, (nm, whi, wlo, hsrc) in enumerate(wmats):
                    sh_toks, sl_toks = prep_tok[(ex, nm)]
                    if nm == 'w2':
                        wb_ = buf(f'mx_{nm}')
                        t_hi = dma('q', whi[:, :], wsp_hi[ex * 3 + mi],
                                   ch=f'mxh{nm}', wait=wb_.w() + sh_toks)
                        t_lo = dma('s', wlo[:, :], wsp_lo[ex * 3 + mi],
                                   ch=f'mxl{nm}', wait=wb_.w() + sl_toks)
                        wb_.setw([t_hi, t_lo])
                        wrd[nm] = [t_hi, t_lo]
                        continue
                    rh = whi[:, :].rearrange("p (c q) -> p c q", q=512)
                    rl = wlo[:, :].rearrange("p (c q) -> p c q", q=512)
                    sh = wsp_hi[ex * 3 + mi].rearrange("p (c q) -> p c q", q=512)
                    sl = wsp_lo[ex * 3 + mi].rearrange("p (c q) -> p c q", q=512)
                    toks = {}
                    t_hi, t_lo = lw_chain[nm]
                    for fj in range(4):
                        wb_ = buf(f'mx_{nm}_f{fj}')
                        t_hi = dma('q', rh[:, :, fj * 128:(fj + 1) * 128],
                                   sh[:, :, fj * 128:(fj + 1) * 128],
                                   ch=f'mxh{nm}',
                                   wait=wb_.w() + sh_toks + [t_hi])
                        t_lo = dma('s', rl[:, :, fj * 128:(fj + 1) * 128],
                                   sl[:, :, fj * 128:(fj + 1) * 128],
                                   ch=f'mxl{nm}',
                                   wait=wb_.w() + sl_toks + [t_lo])
                        wb_.setw([t_hi, t_lo])
                        toks[fj] = [t_hi, t_lo]
                    lw_chain[nm] = [t_hi, t_lo]
                    wrd[nm] = toks
                return wrd

            selx = OR[:, 15136:15392]
            selT = OR[0:16, 0:2048]
            posx = OR[0:16, 2048:4096]
            posT = OR[:, 4096:4352]
            eq_t = OR[:, 4352:4608]
            ework = OR[:, 4608:4864]
            eid_k = OR[:, 4864:4880]
            posk = OR[:, 4880:4896]
            idxk = OR[:, 4896:4912]
            idxku = [OR[:, 4912 + k * 16:4928 + k * 16].bitcast(mybir.dt.uint32)
                     for k in range(4)]
            eidx_sb = OR[:, 4976:5232]
            tok_sb = OR[:, 5232:5248].bitcast(mybir.dt.uint32)
            idxu = R4[:, 0:96].bitcast(mybir.dt.uint32)
            combg = [R4[:, 96:192], R4[:, 192:288]]
            t_csp = dma('s', comb_h[0:T, :].rearrange("(a p) e -> p a e", p=128),
                        comb[:, :].rearrange("p (a e) -> p a e", e=16),
                        ch='combsp', wait=[buf('rt_comb').wr])
            t_eix = dma('q', eidx_sb, H['eidx'][:, :], ch='mxidx', wait=rb_.w())
            t_tok = dma('q', tok_sb, H['tokids'][:, :], ch='mxidx2', wait=rb_.w())
            t_pf = dma('s', idxl_h[:, :].rearrange("(a b) c -> a (b c)", b=768),
                       H['padfill'][:, :], ch='padf')
            t_selx = vop(lambda v: v.tensor_single_scalar(selx, comb[:, :], 0.0, Alu.is_gt),
                         wait=[buf('rt_comb').wr] + buf('rt_selx').w())
            buf('rt_selx').setw(t_selx)
            ev_toks = []
            for grp in range(4):
                bk = grp % 2
                tp_toks = []
                for k in range(4):
                    ts4 = grp * 4 + k
                    t_tp = pb.emit('t', lambda t, b=bk, ts4=ts4, k=k:
                                   t.transpose(psum[b][0:16, k * 128:(k + 1) * 128],
                                               selx[:, ts4 * 16:(ts4 + 1) * 16],
                                               ident[:, :]),
                                   wait=[t_selx] + (pbank[bk].w() if k == 0 else []),
                                   inc=(k == 3))
                    tp_toks.append(t_tp)
                pbank[bk].setw(tp_toks[-1])
                t_ev = vop(lambda v, b=bk, grp=grp:
                           v.tensor_copy(selT[:, grp * 512:(grp + 1) * 512],
                                         psum[b][0:16, 0:512]),
                           wait=[tp_toks[-1], rb_.wr])
                pbank[bk].addr(t_ev)
                ev_toks.append(t_ev)
            t_scan = vop(lambda v: v.tensor_tensor_scan(posx, selT, selT, 0.0,
                                                        Alu.add, Alu.bypass),
                         wait=ev_toks)
            ptp = []
            for ts in range(NTS):
                bk = ts % 2
                t_tp = pb.emit('t', lambda t, b=bk, ts=ts:
                               t.transpose(psum[b][0:128, 0:16],
                                           posx[:, ts * 128:(ts + 1) * 128],
                                           ident[0:16, 0:16]),
                               wait=[t_scan] + pbank[bk].w(), inc=True)
                pbank[bk].setw(t_tp)
                t_ev = vop(lambda v, b=bk, ts=ts:
                           v.tensor_copy(posT[:, ts * 16:(ts + 1) * 16],
                                         psum[b][0:128, 0:16]), wait=t_tp)
                pbank[bk].addr(t_ev)
                ptp.append(t_ev)
            rr16i = lambda ap: ap.rearrange("p (a j) -> p a j", j=16)
            sc_toks = []
            last_round = []
            for k in range(4):
                vkb = v4s[k].rearrange("p (a j) -> p a j", j=1).broadcast_to((128, 16, 16))
                t_eq = vop(lambda v, vb=vkb: v.tensor_tensor(
                    rr16i(eq_t), rr16i(msc0), vb, Alu.is_equal),
                    wait=[buf('rt_comb').wr] + (ptp if k == 0 else []) + last_round)
                t_em = vop(lambda v: v.tensor_tensor(ework, eq_t, eidx_sb, Alu.mult),
                           wait=[t_eq, t_eix])
                t_ei = vop(lambda v: v.reduce_sum(eid_k, rr16i(ework), axis=Ax.X),
                           wait=t_em)
                t_pk = vop(lambda v: v.tensor_tensor(ework, posT, eq_t, Alu.mult),
                           wait=t_ei)
                t_ps = vop(lambda v: v.reduce_sum(posk, rr16i(ework), axis=Ax.X),
                           wait=t_pk)
                t_ix = vop(lambda v: v.tensor_scalar_mul(idxk, posk, 16.0), wait=t_ps)
                t_i2 = vop(lambda v: v.tensor_tensor(idxk, idxk, eid_k, Alu.add), wait=t_ix)
                t_i3 = vop(lambda v: v.tensor_scalar_add(idxk, idxk, -16.0), wait=t_i2)
                t_u = vop(lambda v, k=k: v.tensor_copy(idxku[k][:, :], idxk), wait=t_i3)
                last_round = [t_u]
                for ts in range(NTS):
                    sc_toks.append(pb.emit(
                        'g', lambda g, k=k, ts=ts: g.indirect_dma_start(
                            out=idxl_h[:, :],
                            out_offset=bass.IndirectOffsetOnAxis(
                                ap=idxku[k][:, ts:ts + 1], axis=0),
                            in_=tok_sb[:, ts:ts + 1], in_offset=None),
                        wait=[t_u, t_tok, t_pf], inc=True, dma=True,
                        ch=f'isc{ts % 4}'))
            for nm in ('mx_gs', 'mx_hf', 'mx_osh'):
                buf(nm).addr(last_round[0])
            if index_only:
                return
            t_idxu = dma('q', idxu[:, :].rearrange("p (b e) -> p b e", e=16),
                         idxl_h[:, :].rearrange("(b p e) c -> p b (e c)", b=6, p=128),
                         ch='idxu', wait=sc_toks)

            rsw_tok = {}
            acc_tok = [None]

            def accum_chain(tok_fn, wait):
                t = tok_fn(([acc_tok[0]] if acc_tok[0] else []) + wait)
                acc_tok[0] = t
                return t

            outsh = R3[:, 1024:1536]
            if (16, 'w1') not in prep_tok:
                make_sprep(16)
                make_sprep(17)
                pop_bg(len(bg_q))
            make_prep_steps(0)
            make_prep_steps(1)
            for sx in range(2):
                ex = 16 + sx
                wrd = load_expert_w(ex)
                hbuf = buf('mx_h')
                for tt in range(NTT):
                    h_toks = []
                    for fj in range(4):
                        bg = fj % 2
                        bu = 2 + fj % 2
                        for c in range(NCH):
                            tokg = mm3(psum[bg][:, :],
                                       w1_hi[:, c * 512 + fj * 128:c * 512 + fj * 128 + 128],
                                       w1_lo[:, c * 512 + fj * 128:c * 512 + fj * 128 + 128],
                                       xn_hi[:, c * T + tt * 512:c * T + tt * 512 + 512],
                                       xn_lo[:, c * T + tt * 512:c * T + tt * 512 + 512],
                                       start=(c == 0), stop=(c == NCH - 1),
                                       wait=(pbank[bg].w() + wrd['w1'][fj])
                                       if c == 0 else None)
                        pbank[bg].setw(tokg)
                        buf(f'mx_w1_f{fj}').addr(tokg)
                        for c in range(NCH):
                            toku = mm3(psum[bu][:, :],
                                       w3_hi[:, c * 512 + fj * 128:c * 512 + fj * 128 + 128],
                                       w3_lo[:, c * 512 + fj * 128:c * 512 + fj * 128 + 128],
                                       xn_hi[:, c * T + tt * 512:c * T + tt * 512 + 512],
                                       xn_lo[:, c * T + tt * 512:c * T + tt * 512 + 512],
                                       start=(c == 0), stop=(c == NCH - 1),
                                       wait=(pbank[bu].w() + wrd['w3'][fj])
                                       if c == 0 else None)
                        pbank[bu].setw(toku)
                        buf(f'mx_w3_f{fj}').addr(toku)
                        gsb = buf('mx_gs')
                        t_gs2 = sop(lambda s, b=bg: s.activation(g_s, psum[b][:, :], Act.Silu),
                                    wait=[tokg] + gsb.w())
                        gsb.setw(t_gs2)
                        pbank[bg].addr(t_gs2)
                        hfb = buf('mx_hf')
                        t_h = vop(lambda v, b=bu: v.tensor_tensor(hf32, psum[b][:, :],
                                                                  g_s, Alu.mult),
                                  wait=[toku, t_gs2] + hfb.w())
                        hfb.setw(t_h)
                        gsb.addr(t_h)
                        pbank[bu].addr(t_h)
                        t_sp = split(hf32, hh_hi[:, fj * 512:(fj + 1) * 512],
                                     hh_lo[:, fj * 512:(fj + 1) * 512],
                                     wait=[t_h] + (hbuf.w() if fj == 0 else []))
                        hfb.addr(t_sp)
                        h_toks.append(t_sp)
                        pop_prep(2)
                    hbuf.setw(h_toks)
                    for tl in range(4):
                        ts = tt * 4 + tl
                        for dh in range(2):
                            bk = 6 + (tl * 2 + dh) % 2
                            for fj in range(4):
                                tok = mm3(psum[bk][:, :],
                                          hh_hi[:, fj * 512 + tl * 128:fj * 512 + tl * 128 + 128],
                                          hh_lo[:, fj * 512 + tl * 128:fj * 512 + tl * 128 + 128],
                                          w2_hi[:, fj * 1024 + dh * 512:fj * 1024 + dh * 512 + 512],
                                          w2_lo[:, fj * 1024 + dh * 512:fj * 1024 + dh * 512 + 512],
                                          start=(fj == 0), stop=(fj == 3),
                                          wait=(pbank[bk].w() + h_toks + wrd['w2'])
                                          if fj == 0 else None)
                            pbank[bk].setw(tok)
                            hbuf.addr(tok)
                            buf('mx_w2').addr(tok)
                            osb = buf('mx_osh')
                            t_ev = vop(lambda v, b=bk: v.tensor_copy(outsh, psum[b][:, :]),
                                       wait=[tok] + osb.w())
                            pbank[bk].addr(t_ev)
                            if sx == 0:
                                t_st = dma('s', routed_h[ts * 128:(ts + 1) * 128,
                                                         dh * 512:(dh + 1) * 512],
                                           outsh, ch=f'rsw{(tl * 2 + dh) % 2}',
                                           wait=t_ev)
                                rsw_tok[(ts, dh)] = t_st
                            else:
                                t_st = accum_chain(
                                    lambda w, ts=ts, dh=dh: pb.emit(
                                        'g', lambda g: g.dma_start(
                                            out=routed_h[ts * 128:(ts + 1) * 128,
                                                         dh * 512:(dh + 1) * 512],
                                            in_=outsh, accum_op=Alu.add),
                                        wait=w, inc=True, dma=True, ch='racc'),
                                    [t_ev, rsw_tok[(ts, dh)]])
                            osb.setw(t_ev)
                            osb.addr(t_st)
            buf('mx_xg1').setw([toku])
            buf('wp_stg1').setw([toku])

            xg_f32 = [OR[:, 0:1024], xn_hi[:, :].bitcast(F32)[:, 0:1024]]
            xng_hi = [OR[:, 1024 + i * 6144:1024 + i * 6144 + 3072].bitcast(BF16)
                      for i in range(2)]
            xng_lo = [OR[:, 4096 + i * 6144:4096 + i * 6144 + 3072].bitcast(BF16)
                      for i in range(2)]
            hs_hi = OR[:, 13312:14848].bitcast(BF16)
            hs_lo = OR[:, 14848:16384].bitcast(BF16)
            gs_s = R3[:, 0:768]
            hfs = R3[:, 768:1536]
            oasm = [R3[:, 1536:2560], R3[:, 2560:3584]]

            pf_q = []

            def prefetch_expert(ex):
                pi = ex % 2
                xb2 = buf(f'mx_xng{pi}')
                cgb = buf(f'mx_cg{pi}')
                sp_toks2 = []
                cg_toks = []
                for b6 in range(6):
                    t_c = pb.emit('g', lambda g, b6=b6, ex=ex, pi=pi:
                                  g.indirect_dma_start(
                                      out=combg[pi][:, b6 * 16:(b6 + 1) * 16],
                                      out_offset=None,
                                      in_=comb_h[:, :],
                                      in_offset=bass.IndirectOffsetOnAxis(
                                          ap=idxu[:, b6 * 16 + ex:b6 * 16 + ex + 1],
                                          axis=0)),
                                  wait=([t_idxu, t_csp] + (cgb.w() if b6 == 0 else [])),
                                  inc=True, dma=True, ch=f'mxcg{pi}')
                    cg_toks.append(t_c)
                cgb.setw(cg_toks)

                def blk(b6):
                    gi = (ex * 6 + b6) % 2
                    xgb = buf(f'mx_xg{gi}')
                    t_g = pb.emit('g', lambda g, b6=b6, ex=ex, gi=gi:
                                  g.indirect_dma_start(
                                      out=xg_f32[gi], out_offset=None,
                                      in_=xn_tm[:, :],
                                      in_offset=bass.IndirectOffsetOnAxis(
                                          ap=idxu[:, b6 * 16 + ex:b6 * 16 + ex + 1],
                                          axis=0)),
                                  wait=[t_idxu] + xgb.w(), inc=True, dma=True,
                                  ch=f'mxg{gi}')
                    xgb.setw(t_g)
                    for c in range(NCH):
                        bk = c % 2
                        t_tp = pb.emit('t', lambda t, b=bk, c=c, gi=gi:
                                       t.transpose(psum[b][0:128, (c % 4) * 128:(c % 4) * 128 + 128],
                                                   xg_f32[gi][:, c * 128:(c + 1) * 128],
                                                   ident[:, :]),
                                       wait=[t_g] + pbank[bk].w(), inc=True)
                        pbank[bk].setw(t_tp)
                        xgb.addr(t_tp)
                        col = c * 768 + b6 * 128
                        t_sp2 = split(psum[bk][0:128, (c % 4) * 128:(c % 4) * 128 + 128],
                                      xng_hi[pi][:, col:col + 128],
                                      xng_lo[pi][:, col:col + 128],
                                      wait=[t_tp] + (xb2.w() if (b6 == 0 and c == 0) else []))
                        pbank[bk].addr(t_sp2)
                        sp_toks2.append(t_sp2)
                    if b6 == 5:
                        xb2.setw(list(sp_toks2))

                for b6 in range(6):
                    pf_q.append(lambda b6=b6: blk(b6))
                return sp_toks2, cg_toks

            def pop_pf(n):
                for _ in range(n):
                    if pf_q:
                        pf_q.pop(0)()

            pre = {0: (None, None)}
            pre[0] = prefetch_expert(0)
            pop_pf(6)
            pop_prep(len(prep_q))
            wrd_next = load_expert_w(0)
            for ex in range(16):
                pi = ex % 2
                wrd = wrd_next
                sp_toks2, cg_toks = pre[ex]
                xb2 = buf(f'mx_xng{pi}')
                cgb = buf(f'mx_cg{pi}')
                if ex + 2 < 16:
                    make_prep_steps(ex + 2)
                if ex + 1 < 16:
                    pre[ex + 1] = prefetch_expert(ex + 1)
                h_toks2 = []
                gsb = buf('mx_gs2')
                hfb = buf('mx_hf2')
                hbuf2 = buf('mx_h2')
                for fj in range(4):
                    for tn, (c0, cw) in enumerate(((0, 512), (512, 256))):
                        bg = 2 + tn
                        bu = 4 + tn
                        for c in range(NCH):
                            tokg = mm3(psum[bg][:, 0:cw],
                                       w1_hi[:, c * 512 + fj * 128:c * 512 + fj * 128 + 128],
                                       w1_lo[:, c * 512 + fj * 128:c * 512 + fj * 128 + 128],
                                       xng_hi[pi][:, c * 768 + c0:c * 768 + c0 + cw],
                                       xng_lo[pi][:, c * 768 + c0:c * 768 + c0 + cw],
                                       start=(c == 0), stop=(c == NCH - 1),
                                       wait=(pbank[bg].w() + wrd['w1'][fj] + sp_toks2)
                                       if c == 0 else None)
                        pbank[bg].setw(tokg)
                        buf(f'mx_w1_f{fj}').addr(tokg)
                        xb2.addr(tokg)
                        for c in range(NCH):
                            toku = mm3(psum[bu][:, 0:cw],
                                       w3_hi[:, c * 512 + fj * 128:c * 512 + fj * 128 + 128],
                                       w3_lo[:, c * 512 + fj * 128:c * 512 + fj * 128 + 128],
                                       xng_hi[pi][:, c * 768 + c0:c * 768 + c0 + cw],
                                       xng_lo[pi][:, c * 768 + c0:c * 768 + c0 + cw],
                                       start=(c == 0), stop=(c == NCH - 1),
                                       wait=(pbank[bu].w() + wrd['w3'][fj])
                                       if c == 0 else None)
                        pbank[bu].setw(toku)
                        buf(f'mx_w3_f{fj}').addr(toku)
                        xb2.addr(toku)
                        t_gs2 = sop(lambda s, b=bg, c0=c0, cw=cw:
                                    s.activation(gs_s[:, c0:c0 + cw], psum[b][:, 0:cw],
                                                 Act.Silu),
                                    wait=[tokg] + gsb.w())
                        gsb.setw(t_gs2)
                        pbank[bg].addr(t_gs2)
                        t_h = vop(lambda v, b=bu, c0=c0, cw=cw:
                                  v.tensor_tensor(hfs[:, c0:c0 + cw], psum[b][:, 0:cw],
                                                  gs_s[:, c0:c0 + cw], Alu.mult),
                                  wait=[toku, t_gs2] + hfb.w()
                                  + (buf('mx_osh').w() if ex == 0 and fj == 0
                                     and tn == 0 else []))
                        hfb.setw(t_h)
                        pbank[bu].addr(t_h)
                        gsb.addr(t_h)
                        col = fj * 768 + c0
                        t_sp3 = split(hfs[:, c0:c0 + cw], hs_hi[:, col:col + cw],
                                      hs_lo[:, col:col + cw],
                                      wait=[t_h] + (hbuf2.w() if fj == 0 and tn == 0 else []))
                        hfb.addr(t_sp3)
                        h_toks2.append(t_sp3)
                        pop_prep(3)
                        pop_pf(1)
                hbuf2.setw(h_toks2)
                for b6 in range(6):
                    ob = b6 % 2
                    oab = buf(f'mx_oa{ob}')
                    ev2 = []
                    for dh in range(2):
                        bk = 6 + dh
                        for fj in range(4):
                            tok = mm3(psum[bk][:, :],
                                      hs_hi[:, fj * 768 + b6 * 128:fj * 768 + b6 * 128 + 128],
                                      hs_lo[:, fj * 768 + b6 * 128:fj * 768 + b6 * 128 + 128],
                                      w2_hi[:, fj * 1024 + dh * 512:fj * 1024 + dh * 512 + 512],
                                      w2_lo[:, fj * 1024 + dh * 512:fj * 1024 + dh * 512 + 512],
                                      start=(fj == 0), stop=(fj == 3),
                                      wait=(pbank[bk].w() + h_toks2 + wrd['w2'])
                                      if fj == 0 else None)
                        pbank[bk].setw(tok)
                        hbuf2.addr(tok)
                        buf('mx_w2').addr(tok)
                        t_ev = vop(lambda v, b=bk, dh=dh, ex=ex, b6=b6, pi=pi, ob=ob:
                                   v.tensor_scalar_mul(oasm[ob][:, dh * 512:(dh + 1) * 512],
                                                       psum[b][:, :],
                                                       combg[pi][:, b6 * 16 + ex:b6 * 16 + ex + 1]),
                                   wait=[tok] + cg_toks + oab.w())
                        pbank[bk].addr(t_ev)
                        cgb.addr(t_ev)
                        ev2.append(t_ev)
                    t_sc2 = accum_chain(
                        lambda w, ex=ex, b6=b6, ob=ob: pb.emit(
                            'g', lambda g: g.indirect_dma_start(
                                out=routed_h[:, :],
                                out_offset=bass.IndirectOffsetOnAxis(
                                    ap=idxu[:, b6 * 16 + ex:b6 * 16 + ex + 1], axis=0),
                                in_=oasm[ob], in_offset=None,
                                compute_op=Alu.add),
                            wait=w, inc=True, dma=True, ch='racc'),
                        ev2 + list(rsw_tok.values()))
                    oab.setw(ev2)
                    oab.addr(t_sc2)
                if ex + 1 < 16:
                    pop_prep(len(prep_q))
                    wrd_next = load_expert_w(ex + 1)
            rb_.setw([acc_tok[0]])

        ep_q = []
        ep_done = []

        def queue_ep0ln_steps():
            r_sb = [OR[:, 0:512], OR[:, 512:1024]]
            s_sb = [OR[:, 1024:1536], OR[:, 1536:2048]]
            x_sb = [OR[:, 2048:2560], OR[:, 2560:3072]]
            x1f = [OR[:, 3072:4096], OR[:, 4096:5120]]
            xnf1_ = [OR[:, 5120:6144], OR[:, 7168:8192]]
            gt_ = [OR[:, 6144:6656], OR[:, 6656:7168]]

            def step(ts):
                ib = ts % 2
                x1b = buf(f'e1_x1{ib}')
                t2s = []
                for dh in range(2):
                    ib2 = (ts * 2 + dh) % 2
                    sbf = buf(f'e1_s{ib2}')
                    t_ls = dma('s', s_sb[ib2],
                               s_sp[ts * 128:(ts + 1) * 128, dh * 512:(dh + 1) * 512],
                               ch=f'eps{ib2}', wait=sbf.w())
                    sbf.setw(t_ls)
                    rbf = buf(f'e1_r{ib2}')
                    t_lr = dma('q', r_sb[ib2],
                               routed_h[ts * 128:(ts + 1) * 128, dh * 512:(dh + 1) * 512],
                               ch=f'e0r{ib2}',
                               wait=rbf.w() + (buf('or_routed').w() if ts == 0 else []))
                    rbf.setw(t_lr)
                    xbf = buf(f'e1_x{ib2}')
                    t_lx = dma('g', x_sb[ib2],
                               H['x_in'][ts * 128:(ts + 1) * 128, dh * 512:(dh + 1) * 512],
                               ch=f'epx{ib2}', wait=xbf.w())
                    xbf.setw(t_lx)
                    gbf = buf(f'e1_g{ib2}')
                    t_g = sop(lambda s, ib2=ib2: s.activation(gt_[ib2], r_sb[ib2],
                                                              Act.Silu),
                              wait=[t_lr] + gbf.w())
                    rbf.addr(t_g)
                    t_1 = vop(lambda v, ib2=ib2, ib=ib, dh=dh:
                              v.tensor_tensor(x1f[ib][:, dh * 512:(dh + 1) * 512],
                                              gt_[ib2], s_sb[ib2], Alu.mult),
                              wait=[t_g, t_ls] + (x1b.w() if dh == 0 else []))
                    gbf.setw(t_g)
                    gbf.addr(t_1)
                    sbf.addr(t_1)
                    t_2 = vop(lambda v, ib2=ib2, ib=ib, dh=dh:
                              v.tensor_tensor(x1f[ib][:, dh * 512:(dh + 1) * 512],
                                              x1f[ib][:, dh * 512:(dh + 1) * 512],
                                              x_sb[ib2], Alu.add), wait=[t_1, t_lx])
                    xbf.addr(t_2)
                    t2s.append(t_2)
                t_o = dma('s', x1_sp[ts * 128:(ts + 1) * 128, :], x1f[ib][:, :],
                          ch=f'epo{ib}', wait=t2s)
                ep_done.append(t_o)
                x1b.setw(t2s)
                x1b.addr(t_o)
                k8 = (ts % 4) * 8
                st = stats[:, k8:k8 + 8]
                xnf1 = xnf1_[ib]
                stb = buf(f'e1_stats{ts % 4}')
                xnb = buf(f'e1_xnf{ib}')
                t_sum = vop(lambda v, ib=ib, st=st: v.reduce_sum(st[:, 0:1], x1f[ib],
                                                                 axis=Ax.X),
                            wait=t2s + stb.w())
                t_sq = sop(lambda s, ib=ib, xnf1=xnf1: s.activation(xnf1, x1f[ib],
                                                                    Act.Square),
                           wait=t2s + xnb.w())
                t_ssq = vop(lambda v, st=st, xnf1=xnf1: v.reduce_sum(st[:, 1:2], xnf1,
                                                                    axis=Ax.X),
                            wait=t_sq)
                t_m = vop(lambda v, st=st: v.tensor_scalar_mul(st[:, 2:3], st[:, 0:1],
                                                               1.0 / D), wait=t_sum)
                t_m2 = vop(lambda v, st=st: v.tensor_tensor(st[:, 3:4], st[:, 2:3],
                                                            st[:, 2:3], Alu.mult),
                           wait=t_m)
                t_v = vop(lambda v, st=st: v.scalar_tensor_tensor(st[:, 4:5], st[:, 1:2],
                                                                  1.0 / D, st[:, 3:4],
                                                                  Alu.mult, Alu.subtract),
                          wait=[t_ssq, t_m2])
                t_sd = sop(lambda s, st=st: s.activation(st[:, 5:6], st[:, 4:5], Act.Sqrt,
                                                         bias=st[:, 7:8]), wait=t_v)
                t_r = vop(lambda v, st=st: v.reciprocal(st[:, 6:7], st[:, 5:6]),
                          wait=t_sd)
                stb.setw(t_r)
                t_xn = vop(lambda v, ib=ib, st=st, xnf1=xnf1:
                           v.tensor_scalar(xnf1, x1f[ib], st[:, 2:3], st[:, 6:7],
                                           Alu.subtract, Alu.mult),
                           wait=[t_r] + xnb.w())
                xnb.setw(t_xn)
                x1b.addr(t_xn)
                stb.addr(t_xn)
                sp_list = []
                for c in range(NCH):
                    bk = 6 + c % 2
                    t_tp = pb.emit('t', lambda t, c=c, b=bk, xnf1=xnf1:
                                   t.transpose(psum[b][:, 0:128],
                                               xnf1[:, c * 128:(c + 1) * 128],
                                               ident[:, :]),
                                   wait=[t_xn] + pbank[bk].w(), inc=True)
                    pbank[bk].setw(t_tp)
                    xnb.addr(t_tp)
                    col = c * T + ts * 128
                    t_sp = split(psum[bk][:, 0:128],
                                 xn_hi[:, col:col + 128], xn_lo[:, col:col + 128],
                                 wait=t_tp)
                    pbank[bk].addr(t_sp)
                    sp_list.append(t_sp)
                buf(f'xn_ts{ts}').setw(sp_list)
                ep_done.append(sp_list[-1])

            for ts in range(NTS):
                ep_q.append(lambda ts=ts: step(ts))

        def pop_ep(n):
            for _ in range(n):
                if ep_q:
                    ep_q.pop(0)()

        def phase_ffn1():
            wgb = [R2[:, 0:512].bitcast(BF16), R2[:, 512:1024].bitcast(BF16)]
            wub = [R2[:, 1024:1536].bitcast(BF16), R2[:, 1536:2048].bitcast(BF16)]
            wdb = [R2[:, 2048:2560].bitcast(BF16), R2[:, 2560:3072].bitcast(BF16)]
            h_hi = OR[:, 0:6144].bitcast(BF16)
            h_lo = OR[:, 6144:12288].bitcast(BF16)
            g_s = R3[:, 0:512]
            hf32 = R3[:, 512:1024]
            s_sb = [R3[:, 1024:1536], R3[:, 1536:2048]]
            x_sb = [R3[:, 2048:2560], R3[:, 2560:3072]]
            gt_ = R3[:, 3072:3584]
            orb = buf('or_routed')
            for tt in range(NTT):
                h_toks = []
                for fc in range(NFC):
                    ib = fc % 2
                    wgbuf = buf(f'f1_wg{ib}')
                    t_lg = dma('g', wgb[ib][:, :], H['wg1'][fc], ch=f'f1g{ib}', wait=wgbuf.w())
                    wgbuf.setw(t_lg)
                    wubuf = buf(f'f1_wu{ib}')
                    t_lu = dma('g', wub[ib][:, :], H['wu1'][fc], ch=f'f1u{ib}', wait=wubuf.w())
                    wubuf.setw(t_lu)
                    bg = fc % 2
                    bu = 2 + fc % 2
                    for c in range(NCH):
                        tokg = mm2(psum[bg][:, :], wgb[ib][:, c * 128:(c + 1) * 128],
                                   xn_hi[:, c * T + tt * 512:c * T + tt * 512 + 512],
                                   xn_lo[:, c * T + tt * 512:c * T + tt * 512 + 512],
                                   start=(c == 0), stop=(c == NCH - 1),
                                   wait=(pbank[bg].w() + [t_lg, buf('xn_ready').wr,
                                                          orb.wr])
                                   if c == 0 else None)
                    pbank[bg].setw(tokg)
                    wgbuf.addr(tokg)
                    for c in range(NCH):
                        toku = mm2(psum[bu][:, :], wub[ib][:, c * 128:(c + 1) * 128],
                                   xn_hi[:, c * T + tt * 512:c * T + tt * 512 + 512],
                                   xn_lo[:, c * T + tt * 512:c * T + tt * 512 + 512],
                                   start=(c == 0), stop=(c == NCH - 1),
                                   wait=(pbank[bu].w() + [t_lu]) if c == 0 else None)
                    pbank[bu].setw(toku)
                    wubuf.addr(toku)
                    gsb = buf('f1_gs')
                    t_g = sop(lambda s, b=bg: s.activation(g_s, psum[b][:, :], Act.Silu,
                                                           scale=1.0 / 127.0),
                              wait=[tokg] + gsb.w())
                    gsb.setw(t_g)
                    pbank[bg].addr(t_g)
                    hfb = buf('f1_hf')
                    t_h = vop(lambda v, b=bu: v.scalar_tensor_tensor(
                        hf32, psum[b][:, :], 1.0 / 127.0, g_s, Alu.mult, Alu.mult),
                        wait=[toku, t_g] + hfb.w())
                    hfb.setw(t_h)
                    gsb.addr(t_h)
                    pbank[bu].addr(t_h)
                    hb2 = buf('f1_h')
                    t_sp = split(hf32, h_hi[:, fc * 512:(fc + 1) * 512],
                                 h_lo[:, fc * 512:(fc + 1) * 512],
                                 wait=[t_h] + (hb2.w() if fc == 0 else []))
                    hfb.addr(t_sp)
                    h_toks.append(t_sp)
                buf('f1_h').setw(h_toks)
                dn_tok = [None] * 8
                for fc in range(NFC):
                    ib = fc % 2
                    wdbuf = buf(f'f1_wd{ib}')
                    t_ld = dma('g', wdb[ib][:, :], H['wd1'][:, fc * 1024:(fc + 1) * 1024],
                               ch=f'f1d{ib}', wait=wdbuf.w())
                    wdbuf.setw(t_ld)
                    for tl in range(4):
                        for dh in range(2):
                            bk = tl * 2 + dh
                            fw = None
                            if fc == 0:
                                fw = pbank[bk].w() + h_toks + [t_ld]
                            elif tl == 0 and dh == 0:
                                fw = [t_ld]
                            tok = mm2L(psum[bk][:, :],
                                       h_hi[:, fc * 512 + tl * 128:fc * 512 + tl * 128 + 128],
                                       h_lo[:, fc * 512 + tl * 128:fc * 512 + tl * 128 + 128],
                                       wdb[ib][:, dh * 512:(dh + 1) * 512],
                                       start=(fc == 0), stop=(fc == NFC - 1), wait=fw)
                            if fc == NFC - 1:
                                dn_tok[bk] = tok
                                pbank[bk].setw(tok)
                            buf('f1_h').addr(tok)
                            wdbuf.addr(tok)
                for tl in range(4):
                    ts = tt * 4 + tl
                    for dh in range(2):
                        bk = tl * 2 + dh
                        ib = (tl * 2 + dh) % 2
                        sbf = buf(f'f1_s{ib}')
                        t_ls = dma('s', s_sb[ib],
                                   s_sp[ts * 128:(ts + 1) * 128, dh * 512:(dh + 1) * 512],
                                   ch=f'f1s{ib}', wait=sbf.w())
                        sbf.setw(t_ls)
                        xbf = buf(f'f1_x{ib}')
                        t_lx = dma('s', x_sb[ib],
                                   x1_sp[ts * 128:(ts + 1) * 128, dh * 512:(dh + 1) * 512],
                                   ch=f'f1x{ib}', wait=xbf.w())
                        xbf.setw(t_lx)
                        gbf = buf('f1_gt')
                        t_g = sop(lambda s, b=bk: s.activation(gt_, psum[b][:, :], Act.Silu,
                                                               scale=1.0 / 127.0),
                                  wait=[dn_tok[bk]] + gbf.w())
                        gbf.setw(t_g)
                        pbank[bk].addr(t_g)
                        hfb = buf('f1_hf')
                        t_1 = vop(lambda v, s_=s_sb[ib]: v.tensor_tensor(hf32, gt_, s_,
                                                                         Alu.mult),
                                  wait=[t_g, t_ls] + hfb.w())
                        gbf.addr(t_1)
                        sbf.addr(t_1)
                        t_2 = vop(lambda v, x=x_sb[ib]: v.tensor_tensor(hf32, hf32, x,
                                                                        Alu.add),
                                  wait=[t_1, t_lx])
                        hfb.setw(t_2)
                        xbf.addr(t_2)
                        t_o = dma('s', out_h[ts * 128:(ts + 1) * 128, dh * 512:(dh + 1) * 512],
                                  hf32, ch=f'f1o{(tl * 2 + dh) % 2}', wait=t_2)
                        hfb.addr(t_o)

        ln0_done = []
        if 'ln0' in phases:
            if 'at0' in phases:
                at_stage_cache[(0, 0)] = stage_attn_w(0, 0)
            ln0_done = phase_ln(0)
            if dbg:
                pb.barrier()
            tap('xnhi0', xn_hi[:, :].bitcast(F32))
            tap('xnlo0', xn_lo[:, :].bitcast(F32))
        if 'at0' in phases:
            if 'moe' in phases:
                make_sprep(16)
                make_sprep(17)
            phase_attn(0, o_guard=ln0_done, p_guard=ln0_done)
            pop_bg(len(bg_q))
            pb.barrier()
            tap('O0', OR[:, :])
            tap('s0', s_sp[:, :])
        if 'moe_rt' in phases:
            phase_moe(routing_only=True)
            pb.barrier()
            tap('rt_r3', R3[:, 0:1120])
        if 'moe' in phases:
            phase_moe()
            pb.barrier()
            tap('routed0', routed_h[0:T, :])
            tap('comb', comb[:, :])
        if 'ep0' in phases:
            queue_ep0ln_steps()
            if 'at1' not in phases:
                pop_ep(len(ep_q))
                pb.barrier()
                tap('xnhi1', xn_hi[:, :].bitcast(F32))
        if 'at1' in phases:
            phase_attn(1, o_guard=ep_done)
            pb.barrier()
        if 'f1' in phases:
            phase_ffn1()
        pb.barrier()

        for chn in pb.chcnt:
            sems[chn] = es.enter_context(nc.semaphore('sem_' + chn))
        with nc.Block() as blk:
            blk.tensor(lambda e: [f(e) for f in pb.fns['t']])
            blk.vector(lambda e: [f(e) for f in pb.fns['v']])
            blk.scalar(lambda e: [f(e) for f in pb.fns['s']])
            blk.gpsimd(lambda e: [f(e) for f in pb.fns['g']])
            blk.sync(lambda e: [f(e) for f in pb.fns['q']])
    return nc, pb


_SHARED_CACHE = {}


def kernel(**inputs):
    host = prep_host(inputs)
    nc, pb = build_program()
    x = np.asarray(inputs['L0_attn_x'])
    shared = {k: v for k, v in host.items() if not k.startswith('_')}
    in_maps = []
    for b in range(8):
        m = dict(shared)
        m['x_in'] = np.ascontiguousarray(x[b])
        in_maps.append(m)
    res = run_bass_kernel_spmd(nc, in_maps, core_ids=list(range(8)))
    out = np.stack([res.results[b]['out'] for b in range(8)], axis=0)
    return out.astype(np.float32)


if __name__ == '__main__':
    nc, pb = build_program()
    print({e: len(pb.fns[e]) for e in ENGS})
